# version 05
# baseline (speedup 1.0000x reference)
import math
import numpy as np
import ml_dtypes
from contextlib import ExitStack

import concourse.bass as bass
import concourse.mybir as mybir
from concourse.bass_utils import run_bass_kernel_spmd

F32 = mybir.dt.float32
BF16 = mybir.dt.bfloat16
AF = mybir.ActivationFunctionType
ALU = mybir.AluOpType
AX = mybir.AxisListType.X

B_PER_CORE = 2
T = 2048
D = 1024
H = 16
L = 32
E = 8
FF = 2048
NTOK = B_PER_CORE * T
NTILE = NTOK // 128
RSQL = 1.0 / math.sqrt(L)

ENGINES = ("tensor", "vector", "scalar", "gpsimd", "sync")


class Sem:
    __slots__ = ("h", "n", "name")

    def __init__(self, h, name):
        self.h = h
        self.n = 0
        self.name = name


class Ctx:

    def __init__(self, nc):
        self.nc = nc
        self.q = {e: [] for e in ENGINES}
        self.waited = {e: {} for e in ENGINES}
        self.rsince = {e: set() for e in ENGINES}
        self.wsince = {e: set() for e in ENGINES}
        self.counts = {e: 0 for e in ENGINES}

    def op(self, eng, build, reads=(), writes=(), waits=(), inc=None, inc_by=1,
           is_dma=False):
        q = self.q[eng]
        wmax = {}
        for ev in waits:
            if ev is None:
                continue
            sem, val = ev
            if val > wmax.get(sem, 0):
                wmax[sem] = val
        for sem, val in wmax.items():
            if self.waited[eng].get(sem, 0) >= val:
                continue
            self.waited[eng][sem] = val
            q.append(("wait", sem, val))
        rk = {id(b) for b in reads}
        wk = {id(b) for b in writes}
        rs, ws = self.rsince[eng], self.wsince[eng]
        if (rk & ws) or (wk & ws) or (wk & rs):
            q.append(("drain",))
            rs.clear()
            ws.clear()
        if not is_dma:
            rs |= rk
            ws |= wk
        self.counts[eng] += 1
        if inc is not None:
            inc.n += inc_by
            q.append(("op", build, inc.h, inc_by))
            return (inc, inc.n)
        q.append(("op", build, None, 0))
        return None

    def dma(self, eng, out, in_, reads=(), waits=(), inc=None):
        return self.op(eng, lambda e, o=out, i=in_: e.dma_start(out=o, in_=i),
                       reads=reads, waits=waits, inc=inc, inc_by=16, is_dma=True)

    def wait_only(self, eng, waits):
        wmax = {}
        for ev in waits:
            if ev is None:
                continue
            sem, val = ev
            if val > wmax.get(sem, 0):
                wmax[sem] = val
        for sem, val in wmax.items():
            if self.waited[eng].get(sem, 0) >= val:
                continue
            self.waited[eng][sem] = val
            self.q[eng].append(("wait", sem, val))

    def barrier(self, engines, evs):
        for e in engines:
            self.wait_only(e, evs)

    def replay(self, eng, handle):
        for item in self.q[eng]:
            k = item[0]
            if k == "wait":
                handle.wait_ge(item[1].h, item[2])
            elif k == "drain":
                handle.drain()
            else:
                _, build, sem_h, inc_by = item
                ins = build(handle)
                if sem_h is not None:
                    ins.then_inc(sem_h, inc_by)


class RBuf:

    def __init__(self, tiles):
        self.tiles = list(tiles)
        self.rel = [[] for _ in self.tiles]
        self._next = 0

    def acquire(self):
        s = self._next
        self._next = (self._next + 1) % len(self.tiles)
        w = self.rel[s]
        self.rel[s] = []
        return s, self.tiles[s], w

    def release(self, s, ev):
        if ev is not None:
            self.rel[s].append(ev)

    def seed_all(self, evs):
        for s in range(len(self.tiles)):
            self.rel[s].extend(e for e in evs if e is not None)


def _blkdiag(blocks):
    n = sum(b.shape[0] for b in blocks)
    m = sum(b.shape[1] for b in blocks)
    out = np.zeros((n, m), dtype=blocks[0].dtype)
    r = c = 0
    for b in blocks:
        out[r:r + b.shape[0], c:c + b.shape[1]] = b
        r += b.shape[0]
        c += b.shape[1]
    return out


def build_program(debug=()):
    nc = bass.Bass("TRN2", target_bir_lowering=False)
    es = ExitStack()

    def SB(st, name, shp, dt, side=None):
        return st.enter_context(nc.sbuf_tensor(name, shp, dt, side=side))

    def PS(st, name, shp, dt=F32):
        return st.enter_context(nc.psum_tensor(name, shp, dt))

    dbg = lambda name: "ExternalOutput" if name in debug else "Internal"

    x = nc.dram_tensor("x", [B_PER_CORE, T, D], F32, kind="ExternalInput").ap()
    catT = nc.dram_tensor("catT", [D, 3 * D], F32, kind="ExternalInput").ap()
    catb_fm = nc.dram_tensor("catb_fm", [128, 24], F32, kind="ExternalInput").ap()
    q2lT2_d = nc.dram_tensor("q2lT2", [128, 64], F32, kind="ExternalInput").ap()
    kv2lT2_d = nc.dram_tensor("kv2lT2", [128, 64], F32, kind="ExternalInput").ap()
    q2lb2_d = nc.dram_tensor("q2lb2", [64, 1], F32, kind="ExternalInput").ap()
    kv2lb2_d = nc.dram_tensor("kv2lb2", [64, 1], F32, kind="ExternalInput").ap()
    kv2lb2v_d = nc.dram_tensor("kv2lb2v", [1, 64], F32, kind="ExternalInput").ap()
    wqkT4_d = nc.dram_tensor("wqkT4", [128, 128], F32, kind="ExternalInput").ap()
    wqkb4_d = nc.dram_tensor("wqkb4", [128, 1], F32, kind="ExternalInput").ap()
    protT4_d = nc.dram_tensor("protT4", [128, 128], F32, kind="ExternalInput").ap()
    l2oT2_d = nc.dram_tensor("l2oT2", [128, 128], F32, kind="ExternalInput").ap()
    l2ob2_d = nc.dram_tensor("l2ob2", [128, 1], F32, kind="ExternalInput").ap()
    cprojT_d = nc.dram_tensor("cprojT", [D, D], F32, kind="ExternalInput").ap()
    cprojb_d = nc.dram_tensor("cprojb", [1, D], F32, kind="ExternalInput").ap()
    router_d = nc.dram_tensor("router_w", [D, E], F32, kind="ExternalInput").ap()
    wg_d = nc.dram_tensor("wg", [E, D, FF], F32, kind="ExternalInput").ap()
    wu_d = nc.dram_tensor("wu", [E, D, FF], F32, kind="ExternalInput").ap()
    wd_d = nc.dram_tensor("wd", [E, FF, D], F32, kind="ExternalInput").ap()
    cos4_d = nc.dram_tensor("cos4", [128, T], F32, kind="ExternalInput").ap()
    sin4_d = nc.dram_tensor("sin4", [128, T], F32, kind="ExternalInput").ap()
    
    ident_f_d = nc.dram_tensor("ident_f", [128, 128], F32, kind="ExternalInput").ap()
    maskdiag_d = nc.dram_tensor("maskdiag", [128, 128], F32, kind="ExternalInput").ap()

    xn_hbm = nc.dram_tensor("xn_hbm", [8, 128, NTOK], BF16, kind=dbg("xn_hbm")).ap()
    attn_acc = nc.dram_tensor("attn_acc", [NTOK, D], F32, kind=dbg("attn_acc")).ap()
    out_d = nc.dram_tensor("out", [B_PER_CORE, T, D], F32, kind="ExternalOutput").ap()
    fullw_dbg = None
    if "fullw" in debug:
        fullw_dbg = nc.dram_tensor("fullw", [128, NTILE * 8], F32,
                                   kind="ExternalOutput").ap()

    S = {n: Sem(es.enter_context(nc.semaphore(n)), n)
         for n in ("dX", "dN", "dW", "dA", "dC", "dD", "dV",
                   "sT", "sV", "sS", "sG", "stO")}
    dX, dN, dW, dA, dC = S["dX"], S["dN"], S["dW"], S["dA"], S["dC"]
    dD, dV = S["dD"], S["dV"]
    sT, sV, sS, sG, stO = S["sT"], S["sV"], S["sS"], S["sG"], S["stO"]

    ctx = Ctx(nc)

    fullw_all = SB(es, "fullw_all", [128, NTILE * 8], F32)
    ident_bf = SB(es, "ident_bf_sb", [128, 128], BF16)
    ident_f = SB(es, "ident_f_sb", [128, 128], F32)
    ones_f = SB(es, "ones_f", [128, 128], F32)
    maskdiag = SB(es, "maskdiag_sb", [128, 128], BF16)
    catb_sb = SB(es, "catb_sb", [128, 24], F32)
    q2lT2_bf = SB(es, "q2lT2_bf", [128, 64], BF16)
    kv2lT2_bf = SB(es, "kv2lT2_bf", [128, 64], BF16)
    q2lb2 = SB(es, "q2lb2_sb", [64, 1], F32)
    kv2lb2 = SB(es, "kv2lb2_sb", [64, 1], F32)
    kv2lb2v = SB(es, "kv2lb2v_sb", [1, 64], F32)
    wqkT4_bf = SB(es, "wqkT4_bf", [128, 128], BF16)
    wqkb4 = SB(es, "wqkb4_sb", [128, 1], F32)
    protT4 = SB(es, "protT4_bf", [128, 128], BF16)
    l2oT2_bf = SB(es, "l2oT2_bf", [128, 128], BF16)
    l2ob2 = SB(es, "l2ob2_sb", [128, 1], F32)
    router_sb = SB(es, "router_sb", [128, 8 * E], F32)
    xn_sb = RBuf([SB(es, f"xn_sb{i}", [128, 8 * 512], BF16) for i in range(2)])
    ws_g = RBuf([SB(es, f"ws_g{i}", [128, 1024], F32) for i in range(2)])
    ws_u = RBuf([SB(es, f"ws_u{i}", [128, 1024], F32) for i in range(2)])
    ws_d = RBuf([SB(es, f"ws_d{i}", [128, 1024], F32) for i in range(2)])
    acc_io = RBuf([SB(es, f"acc_io{i}", [128, D], F32) for i in range(4)])
    catw_bf = RBuf([SB(es, f"catw_bf{i}", [128, 3 * FF], BF16) for i in range(2)])

    es1 = ExitStack()
    smalls_f = SB(es1, "smalls_f", [128, 640], F32)
    xin = RBuf([SB(es1, f"xin{i}", [128, D], F32) for i in range(2)])
    for dst, src in ((ident_f, ident_f_d), (catb_sb, catb_fm),
                     (q2lb2, q2lb2_d), (kv2lb2, kv2lb2_d), (kv2lb2v, kv2lb2v_d),
                     (wqkb4, wqkb4_d), (l2ob2, l2ob2_d)):
        ctx.dma("sync", dst[:], src, inc=dW)
    for dc in range(8):
        ctx.dma("sync", router_sb[:, dc * 8:(dc + 1) * 8],
                router_d[dc * 128:(dc + 1) * 128, :], inc=dW)
    sm_off = {}
    off = 0
    for name, src, w in (("q2lT2", q2lT2_d, 64), ("kv2lT2", kv2lT2_d, 64),
                         ("wqkT4", wqkT4_d, 128), ("l2oT2", l2oT2_d, 128),
                         ("protT4", protT4_d, 128), ("maskdiag", maskdiag_d, 128)):
        sm_off[name] = (off, w)
        ctx.dma("sync", smalls_f[:, off:off + w], src, inc=dW)
        off += w
    ld_done = (dW, dW.n)

    ctx.op("vector", lambda e: e.memset(ones_f[:], 1.0), writes=(ones_f,))
    ev_smalls = None
    ctx.op("vector", lambda e: e.tensor_copy(ident_bf[:], ident_f[:]),
           reads=(ident_f,), writes=(ident_bf,), waits=[ld_done])
    for name, dst in (("q2lT2", q2lT2_bf), ("kv2lT2", kv2lT2_bf),
                      ("wqkT4", wqkT4_bf), ("l2oT2", l2oT2_bf),
                      ("protT4", protT4), ("maskdiag", maskdiag)):
        o, w = sm_off[name]
        ev_smalls = ctx.op("vector",
            lambda e, d=dst, src=smalls_f[:, o:o + w]: e.tensor_copy(d[:], src),
            reads=(smalls_f,), writes=(dst,), waits=[ld_done], inc=sV)

    xn32 = RBuf([SB(es1, f"xn32_{i}", [128, D], F32) for i in range(2)])
    xnbf = RBuf([SB(es1, f"xnbf_{i}", [128, D], BF16) for i in range(2)])
    xnfm32 = RBuf([SB(es1, f"xnfm32_{i}", [128, D], F32) for i in range(2)])
    statsb = RBuf([SB(es1, f"stats_{i}", [128, 8], F32) for i in range(2)])
    acc8 = SB(es1, "xnfm_acc", [128, 8 * 1024], BF16)
    r8b = RBuf([SB(es1, f"r8_{i}", [128, 8], F32) for i in range(2)])
    rtmp = SB(es1, "rtmp", [128, 48], F32)
    p_trb = PS(es1, "p_trb", [128, 1024], BF16)
    p_trf = PS(es1, "p_trf", [128, 1024], F32)
    p_log = RBuf([PS(es1, f"p_log{i}", [128, 8]) for i in range(2)])

    acc8_rel = []
    p1_defer = []
    xn_group_ev = [None] * (NTILE // 8)
    fullw_ready = [None] * NTILE
    trb_rel = None
    trf_rel = None

    for ti in range(NTILE):
        s0, t0 = divmod(ti * 128, T)
        xsl, xt, xw = xin.acquire()
        ev_x = ctx.dma("sync", xt[:], x[s0, t0:t0 + 128, :], waits=xw, inc=dX)

        ssl, st_, stw = statsb.acquire()
        bsl, xbf, xbfw = xnbf.acquire()
        ctx.op("vector", lambda e, d=st_[:, 1:2]: e.memset(d, 0.0),
               reads=(), writes=(st_,), waits=stw)
        ctx.op("vector", lambda e, xt=xt, xb=xbf, d=st_[:, 1:2]:
               e.scalar_tensor_tensor(xb[:], xt[:], 0.0, xt[:], ALU.add, ALU.mult,
                                      accum_out=d),
               reads=(xt, st_), writes=(xbf, st_), waits=[ev_x] + xbfw)
        ctx.op("vector", lambda e, xt=xt, d=st_[:, 0:1]: e.reduce_sum(d, xt[:], axis=AX),
               reads=(xt,), writes=(st_,))
        ctx.op("vector", lambda e, s=st_[:, 0:1], d=st_[:, 2:3]:
               e.tensor_scalar(d, s, -1.0 / D, None, ALU.mult),
               reads=(st_,), writes=(st_,))
        ctx.op("vector", lambda e, s=st_[:, 2:3], d=st_[:, 3:4]:
               e.tensor_tensor(d, s, s, ALU.mult),
               reads=(st_,), writes=(st_,))
        ctx.op("vector", lambda e, s=st_[:, 1:2], d=st_[:, 4:5]:
               e.tensor_scalar(d, s, 1.0 / D, 1e-5, ALU.mult, ALU.add),
               reads=(st_,), writes=(st_,))
        ev_var = ctx.op("vector", lambda e, a=st_[:, 4:5], b=st_[:, 3:4]:
               e.tensor_tensor(a, a, b, ALU.subtract),
               reads=(st_,), writes=(st_,), inc=sV)
        ctx.op("scalar", lambda e, i=st_[:, 4:5], o=st_[:, 5:6]:
               e.activation(o, i, AF.Ln),
               reads=(st_,), writes=(st_,), waits=[ev_var])
        ev_rstd = ctx.op("scalar", lambda e, i=st_[:, 5:6], o=st_[:, 6:7]:
               e.activation(o, i, AF.Exp, scale=-0.5),
               reads=(st_,), writes=(st_,), inc=sS)

        nsl, x32, x32w = xn32.acquire()
        ev_xn32 = ctx.op("vector", lambda e, xt=xt, o=x32, m=st_[:, 2:3], r=st_[:, 6:7]:
               e.tensor_scalar(o[:], xt[:], m, r, ALU.add, ALU.mult),
               reads=(xt, st_), writes=(x32,), waits=[ev_rstd] + x32w, inc=sV)
        ev_xnbf = ctx.op("vector", lambda e, i=x32, o=xbf: e.tensor_copy(o[:], i[:]),
               reads=(x32,), writes=(xbf,), inc=sV)

        for c in range(8):
            ctx.op("tensor", lambda e, i=xbf[:, c * 128:(c + 1) * 128],
                   o=p_trb[:, c * 128:(c + 1) * 128]: e.transpose(o, i, ident_bf[:]),
                   waits=[ev_xnbf, ld_done] + ([trb_rel] if c == 0 else []),
                   inc=sT if c == 7 else None)
        ev_trb = (sT, sT.n)
        for c in range(8):
            ctx.op("tensor", lambda e, i=x32[:, c * 128:(c + 1) * 128],
                   o=p_trf[:, c * 128:(c + 1) * 128]: e.transpose(o, i, ident_f[:]),
                   waits=[ev_xn32] + ([trf_rel] if c == 0 else []),
                   inc=sT if c == 7 else None)
        ev_trf = (sT, sT.n)
        xnbf.release(bsl, ev_trb)
        xn32.release(nsl, ev_trf)

        g8 = ti % 8
        aw = []
        if g8 == 0:
            aw, acc8_rel = acc8_rel, []
        ev_accw = ctx.op("vector",
            lambda e, o=acc8[:].rearrange("p (c t) -> p c t", c=8)[:, :, g8 * 128:(g8 + 1) * 128],
            i=p_trb[:].rearrange("p (c t) -> p c t", c=8): e.tensor_copy(o, i),
            reads=(p_trb,), writes=(acc8,), waits=[ev_trb] + aw, inc=sV)
        trb_rel = ev_accw

        fsl, xf32, xf32w = xnfm32.acquire()
        ev_xf32 = ctx.op("scalar", lambda e, o=xf32: e.copy(o[:], p_trf[:]),
               reads=(p_trf,), writes=(xf32,), waits=[ev_trf] + xf32w, inc=sS)
        trf_rel = ev_xf32

        if g8 == 7:
            gi = ti // 8
            ev_g = ctx.dma("gpsimd",
                xn_hbm[:, :, gi * 1024:(gi + 1) * 1024].transpose([1, 0, 2]),
                acc8[:].rearrange("p (c t) -> p c t", c=8),
                waits=[ev_accw], inc=dN)
            xn_group_ev[gi] = ev_g
            acc8_rel.append(ev_g)

        lsl, plog, plogw = p_log.acquire()
        for dc in range(8):
            ctx.op("tensor", lambda e, pl=plog, lh=xf32[:, dc * 128:(dc + 1) * 128],
                   rh=router_sb[:, dc * 8:(dc + 1) * 8], st=(dc == 0), sp=(dc == 7):
                   e.matmul(pl[:], lh, rh, start=st, stop=sp),
                   waits=[ev_xf32, ld_done] + (plogw if dc == 0 else []),
                   inc=sT if dc == 7 else None)
        ev_logit = (sT, sT.n)
        xnfm32.release(fsl, ev_logit)

        def p1_epi(ti=ti, plog=plog, lsl=lsl, ev_logit=ev_logit):
            r8sl, r8, r8w = r8b.acquire()
            ro = (ti % 2) * 24
            c0 = rtmp[:, ro + 0:ro + 1]
            c1 = rtmp[:, ro + 1:ro + 2]
            c2 = rtmp[:, ro + 2:ro + 3]
            c3 = rtmp[:, ro + 3:ro + 4]
            c4 = rtmp[:, ro + 4:ro + 5]
            c5 = rtmp[:, ro + 5:ro + 6]
            v8 = rtmp[:, ro + 8:ro + 16]
            a8 = rtmp[:, ro + 16:ro + 24]
            ev1 = ctx.op("vector", lambda e, pl=plog, d=c0: e.reduce_max(d, pl[:], axis=AX,
                   negate=True),
                   reads=(plog, rtmp), writes=(rtmp,), waits=[ev_logit] + r8w, inc=sV)
            ev_p = ctx.op("scalar", lambda e, pl=plog, r8=r8, b=c0: e.activation(
                   r8[:], pl[:], AF.Exp, bias=b),
                   reads=(plog, rtmp, r8), writes=(r8,), waits=[ev1], inc=sS)
            p_log.release(lsl, ev_p)
            ctx.op("vector", lambda e, r8=r8, d=c1: e.reduce_sum(d, r8[:], axis=AX),
                   reads=(r8, rtmp), writes=(rtmp,), waits=[ev_p])
            ctx.op("vector", lambda e, d=c1: e.reciprocal(d, d),
                   reads=(rtmp,), writes=(rtmp,))
            ctx.op("vector", lambda e, r8=r8, s=c1: e.tensor_scalar(
                   r8[:], r8[:], s, None, ALU.mult),
                   reads=(r8, rtmp), writes=(r8,))
            ctx.op("vector", lambda e, pl=plog, d=v8, s=c0: e.tensor_scalar(
                   d, pl[:], s, None, ALU.add),
                   reads=(plog, rtmp), writes=(rtmp,))
            ctx.op("vector", lambda e, d=a8, s=v8: e.tensor_scalar(
                   d, s, 0.0, None, ALU.is_equal),
                   reads=(rtmp,), writes=(rtmp,))
            ctx.op("vector", lambda e, pl=plog, d=v8, i1=a8: e.scalar_tensor_tensor(
                   d, i1, -1e30, pl[:], ALU.mult, ALU.add),
                   reads=(rtmp, plog), writes=(rtmp,))
            ctx.op("vector", lambda e, d=c2, s=v8: e.reduce_max(d, s, axis=AX, negate=True),
                   reads=(rtmp,), writes=(rtmp,))
            ctx.op("vector", lambda e, d=v8, s=c2: e.tensor_scalar(d, d, s, None, ALU.add),
                   reads=(rtmp,), writes=(rtmp,))
            ctx.op("vector", lambda e, d=v8: e.tensor_scalar(d, d, 0.0, None, ALU.is_equal),
                   reads=(rtmp,), writes=(rtmp,))
            ctx.op("vector", lambda e, r8=r8, i1=a8: e.tensor_tensor(
                   i1, i1, r8[:], ALU.mult),
                   reads=(rtmp, r8), writes=(rtmp,))
            ctx.op("vector", lambda e, s=a8, d=c3: e.reduce_sum(d, s, axis=AX),
                   reads=(rtmp,), writes=(rtmp,))
            ctx.op("vector", lambda e, r8=r8, i2=v8: e.tensor_tensor(
                   r8[:], r8[:], i2, ALU.mult),
                   reads=(rtmp, r8), writes=(r8,))
            ctx.op("vector", lambda e, r8=r8, d=c4: e.reduce_sum(d, r8[:], axis=AX),
                   reads=(r8, rtmp), writes=(rtmp,))
            ev_d = ctx.op("vector", lambda e, a=c4, b=c3: e.tensor_tensor(
                   a, a, b, ALU.subtract),
                   reads=(rtmp,), writes=(rtmp,), inc=sV)
            ev_e2 = ctx.op("scalar", lambda e, i=c4, o=c5: e.activation(o, i, AF.Exp),
                   reads=(rtmp,), writes=(rtmp,), waits=[ev_d], inc=sS)
            ctx.op("vector", lambda e, i=c5, o=c4: e.tensor_scalar(o, i, 1.0, None, ALU.add),
                   reads=(rtmp,), writes=(rtmp,), waits=[ev_e2])
            ctx.op("vector", lambda e, d=c4: e.reciprocal(d, d),
                   reads=(rtmp,), writes=(rtmp,))
            ctx.op("vector", lambda e, i=c4, o=c5: e.tensor_scalar(
                   o, i, -1.0, 1.0, ALU.mult, ALU.add),
                   reads=(rtmp,), writes=(rtmp,))
            ctx.op("vector", lambda e, i=c3, o=c2: e.reciprocal(o, i),
                   reads=(rtmp,), writes=(rtmp,))
            ctx.op("vector", lambda e, a=c2, b=c4: e.tensor_tensor(a, a, b, ALU.mult),
                   reads=(rtmp,), writes=(rtmp,))
            ctx.op("vector", lambda e, pi1=a8, s=c2,
                   d=fullw_all[:, ti * 8:(ti + 1) * 8]: e.tensor_scalar(
                   d, pi1, s, None, ALU.mult),
                   reads=(rtmp,), writes=(fullw_all,))
            ev_fw = ctx.op("vector", lambda e, i2=v8, nw2=c5,
                   d=fullw_all[:, ti * 8:(ti + 1) * 8]: e.scalar_tensor_tensor(
                   d, i2, nw2, d, ALU.mult, ALU.add),
                   reads=(rtmp, fullw_all), writes=(fullw_all,), inc=sV)
            fullw_ready[ti] = ev_fw
            r8b.release(r8sl, ev_fw)
        p1_defer.append(p1_epi)
        if len(p1_defer) > 1:
            p1_defer.pop(0)()
        statsb.release(ssl, ev_xn32)
        xin.release(xsl, ev_xn32)

    while p1_defer:
        p1_defer.pop(0)()

    if fullw_dbg is not None:
        ctx.dma("gpsimd", fullw_dbg, fullw_all[:], waits=list(fullw_ready), inc=stO)

    p1_end = [(sV, sV.n), (sS, sS.n), (sT, sT.n), (sG, sG.n)]
    es1.close()
    es3 = ExitStack()
    wg_bf = [SB(es3, f"wg_bf{p}", [128, 8 * 1024], BF16) for p in range(2)]
    wu_bf = [SB(es3, f"wu_bf{p}", [128, 8 * 1024], BF16) for p in range(2)]
    wd_bf = [SB(es3, f"wd_bf{p}", [128, 8 * 1024], BF16) for p in range(2)]
    fuse_bf2 = [SB(es3, f"fuse_bf{i}", [128, 8 * 512], BF16) for i in range(2)]
    silu_sb = RBuf([SB(es3, f"silu_sb{i}", [128, 512], BF16) for i in range(2)])
    out_sb = RBuf([SB(es3, f"out_sb{i}", [128, D], F32) for i in range(2)])

    pool3 = [PS(es3, f"p3bank{i}", [128, 512]) for i in range(6)]
    p_g = RBuf([pool3[0], pool3[1]])
    p_u = RBuf([pool3[2], pool3[3]])
    p_d = RBuf([pool3[4], pool3[5]])
    p1_psum_readers = [(sV, sV.n), (sS, sS.n)]
    p_g.seed_all(p1_psum_readers)
    p_u.seed_all(p1_psum_readers)
    p_d.seed_all(p1_psum_readers)

    wmat_rel = {("g", 0): [], ("g", 1): [], ("u", 0): [], ("u", 1): [],
                ("d", 0): [], ("d", 1): []}
    fuse_rel2 = [[], []]
    fuse_evs_by_k = {}
    acc_tile_ev = [None] * NTILE
    NK = 16 * 8
    conv_by_pass = {}

    def rec_w_chunk(p34, mat, ch):
        e_i, hf = divmod(p34, 2)
        par = p34 % 2
        src3, dstb, q_eng, q_sem, wsp = {
            "g": (wg_d, wg_bf[par], "sync", dW, ws_g),
            "u": (wu_d, wu_bf[par], "scalar", dC, ws_u),
            "d": (wd_d, wd_bf[par], "gpsimd", dD, ws_d)}[mat]
        if mat == "d":
            src_ap = src3[e_i, hf * 1024 + ch * 128: hf * 1024 + ch * 128 + 128, :]
        else:
            src_ap = src3[e_i, ch * 128:(ch + 1) * 128, hf * 1024:(hf + 1) * 1024]
        wsl, wst, wstw = wsp.acquire()
        ev_l = ctx.dma(q_eng, wst[:], src_ap,
                       waits=wstw + (wmat_rel[(mat, par)] if ch == 0 else []),
                       inc=q_sem)
        ev_c = ctx.op("gpsimd", lambda e, i=wst,
            o=dstb[:, ch * 1024:(ch + 1) * 1024]: e.tensor_copy(o, i[:]),
            waits=[ev_l] + (p1_end if p34 == 0 and ch == 0 and mat == "g" else []),
            inc=sG)
        wsp.release(wsl, ev_c)
        conv_by_pass.setdefault(p34, {}).setdefault(mat, []).append(ev_c)

    chunk_order = [(m, c) for c in range(8) for m in ("g", "u", "d")]
    for m, c in chunk_order:
        rec_w_chunk(0, m, c)

    def emit_xn_load(k):
        nsl, xnt, xnw = xn_sb.acquire()
        tc = k % 8
        ev_xn = None
        for c in range(8):
            ev_xn = ctx.dma("sync", xnt[:, c * 512:(c + 1) * 512],
                xn_hbm[c, :, tc * 512: tc * 512 + 512],
                waits=[xn_group_ev[(tc * 512) // 1024]] + (xnw if c == 0 else []),
                inc=dN)
        return (nsl, xnt, ev_xn)

    def emit_acc_prefetch(k):
        p34, tc = divmod(k, 8)
        out = []
        for sub in range(4):
            ti = tc * 4 + sub
            asl, ai, aiw = acc_io.acquire()
            if p34 == 0:
                s0, t0 = divmod(ti * 128, T)
                ev_ai = ctx.dma("scalar", ai[:], x[s0, t0:t0 + 128, :],
                                waits=aiw, inc=dV)
            else:
                ev_ai = ctx.dma("scalar", ai[:], attn_acc[ti * 128:(ti + 1) * 128, :],
                                waits=[acc_tile_ev[ti]] + aiw, inc=dV)
            out.append((asl, ai, ev_ai))
        return out

    def emit_gu(k, xn_entry):
        p34, tc = divmod(k, 8)
        par = p34 % 2
        conv = conv_by_pass[p34]
        fpar = k % 2
        fb = fuse_bf2[fpar]
        frel, fuse_rel2[fpar] = fuse_rel2[fpar], []
        nsl, xnt, ev_xn = xn_entry
        fuse_evs = []
        for fc in range(8):
            gsl, pg, pgw = p_g.acquire()
            usl, pu, puw = p_u.acquire()
            for dc in range(8):
                ctx.op("tensor", lambda e, pg=pg,
                    lh=wg_bf[par][:, dc * 1024 + fc * 128: dc * 1024 + fc * 128 + 128],
                    rh=xnt[:, dc * 512:(dc + 1) * 512],
                    st=(dc == 0), sp=(dc == 7): e.matmul(
                    pg[:], lh, rh, start=st, stop=sp),
                    waits=([ev_xn, conv["g"][7]] + pgw) if dc == 0 else [],
                    inc=sT if dc == 7 else None)
            ev_g = (sT, sT.n)
            for dc in range(8):
                ctx.op("tensor", lambda e, pu=pu,
                    lh=wu_bf[par][:, dc * 1024 + fc * 128: dc * 1024 + fc * 128 + 128],
                    rh=xnt[:, dc * 512:(dc + 1) * 512],
                    st=(dc == 0), sp=(dc == 7): e.matmul(
                    pu[:], lh, rh, start=st, stop=sp),
                    waits=([conv["u"][7]] + puw) if dc == 0 else [],
                    inc=sT if dc == 7 else None)
            ev_u = (sT, sT.n)
            ssl, sl, slw = silu_sb.acquire()
            ev_sl = ctx.op("scalar", lambda e, pg=pg, sl=sl: e.activation(
                sl[:], pg[:], AF.Silu),
                reads=(pg, sl), writes=(sl,), waits=[ev_g] + slw, inc=sS)
            p_g.release(gsl, ev_sl)
            ev_f = ctx.op("vector", lambda e, pu=pu, sl=sl,
                o=fb[:, fc * 512:(fc + 1) * 512]: e.tensor_tensor(
                o, sl[:], pu[:], ALU.mult),
                reads=(sl, pu, fb), writes=(fb,),
                waits=[ev_u, ev_sl] + (frel if fc == 0 else []), inc=sV)
            p_u.release(usl, ev_f)
            silu_sb.release(ssl, ev_f)
            fuse_evs.append(ev_f)
        xn_sb.release(nsl, (sT, sT.n))
        fuse_evs_by_k[k] = fuse_evs
        if tc == 7:
            ev_gu_done = (sT, sT.n)
            wmat_rel[("g", par)] = [ev_gu_done]
            wmat_rel[("u", par)] = [ev_gu_done]

    def emit_down(k, acc_ins):
        p34, tc = divmod(k, 8)
        par = p34 % 2
        e_i = p34 // 2
        fpar = k % 2
        fb = fuse_bf2[fpar]
        fuse_evs = fuse_evs_by_k.pop(k)
        conv = conv_by_pass[p34]
        for sub in range(4):
            ti = tc * 4 + sub
            asl, ai, ev_ai = acc_ins[sub]
            osl, oo, oow = out_sb.acquire()
            ev_o_half = [None, None]
            for dhalf in range(2):
                dsl, pd, pdw = p_d.acquire()
                for fc in range(8):
                    ctx.op("tensor", lambda e, pd=pd,
                        lh=fb[:, fc * 512 + sub * 128: fc * 512 + sub * 128 + 128],
                        rh=wd_bf[par][:, fc * 1024 + dhalf * 512: fc * 1024 + dhalf * 512 + 512],
                        st=(fc == 0), sp=(fc == 7): e.matmul(
                        pd[:], lh, rh, start=st, stop=sp),
                        waits=([fuse_evs[fc]] +
                               (conv["d"][7:] if fc == 0 and dhalf == 0 and sub == 0 else []) +
                               (pdw if fc == 0 else [])),
                        inc=sT if fc == 7 else None)
                ev_dmm = (sT, sT.n)
                ev_o = ctx.op("vector", lambda e, pd=pd, ai=ai, oo=oo, dhalf=dhalf,
                    wcol=fullw_all[:, ti * 8 + e_i: ti * 8 + e_i + 1]:
                    e.scalar_tensor_tensor(
                    oo[:, dhalf * 512:(dhalf + 1) * 512], pd[:], wcol,
                    ai[:, dhalf * 512:(dhalf + 1) * 512], ALU.mult, ALU.add),
                    reads=(pd, ai, fullw_all, oo), writes=(oo,),
                    waits=[ev_dmm, ev_ai, fullw_ready[ti]] +
                          (oow if dhalf == 0 else []),
                    inc=sV)
                ev_o_half[dhalf] = ev_o
                p_d.release(dsl, ev_o)
            acc_io.release(asl, ev_o_half[1])
            ev_w = ctx.dma("gpsimd", attn_acc[ti * 128:(ti + 1) * 128, :], oo[:],
                           reads=(oo,), waits=ev_o_half, inc=dA)
            acc_tile_ev[ti] = ev_w
            out_sb.release(osl, ev_w)
        fuse_rel2[fpar] = [(sT, sT.n)]
        if tc == 7:
            wmat_rel[("d", par)] = [(sT, sT.n)]

    xn_cur = emit_xn_load(0)
    for k in range(NK):
        p34, tc = divmod(k, 8)
        acc_pf = emit_acc_prefetch(k - 1) if k > 0 else None
        xn_next = emit_xn_load(k + 1) if k + 1 < NK else None
        emit_gu(k, xn_cur)
        xn_cur = xn_next
        if k > 0:
            emit_down(k - 1, acc_pf)
        if tc < 2 and p34 + 1 < 2 * E:
            for m, c in chunk_order[tc * 12:(tc + 1) * 12]:
                rec_w_chunk(p34 + 1, m, c)
    emit_down(NK - 1, emit_acc_prefetch(NK - 1))

    p3_end = [(sV, sV.n), (sS, sS.n), (sT, sT.n), (sG, sG.n)]
    es3.close()

    ctx.barrier(("tensor", "vector", "scalar"), p3_end)

    es2 = ExitStack()
    v_aug = SB(es2, "v_aug", [128, 16 * 528], BF16)
    cprojT_bf = SB(es2, "cprojT_bf", [128, 8 * 1024], BF16)
    qkv_sb = RBuf([SB(es2, f"qkv_sb{i}", [128, 512], BF16) for i in range(3)])
    ql_sb = RBuf([SB(es2, f"ql_sb{i}", [64, 512], BF16) for i in range(2)])
    rope_f = RBuf([SB(es2, f"rope_f{i}", [64, 1024], F32) for i in range(2)])
    q4tmp = RBuf([SB(es2, f"q4tmp{i}", [128, 512], BF16) for i in range(1)])
    exp_sb = RBuf([SB(es2, f"exp_sb{i}", [128, 512], BF16) for i in range(3)])
    rcp_sb = RBuf([SB(es2, f"rcp_sb{i}", [33, 512], F32) for i in range(1)])
    bcast_sb = RBuf([SB(es2, f"bcast_sb{i}", [32, 512], F32) for i in range(2)])
    cos4 = SB(es2, "cos4_sb", [128, T], F32)
    sin4 = SB(es2, "sin4_sb", [128, T], F32)
    cprojb_sb = SB(es2, "cprojb_sb", [1, D], F32)

    pool = [PS(es2, f"bank{i}", [128, 512]) for i in range(8)]
    p_qkv = RBuf([pool[0], pool[1], pool[2]])
    p_lat = RBuf([pool[3], pool[7]])
    p_s = RBuf([pool[4], pool[5], pool[6]])
    p_yh = [pool[0], pool[1], pool[2], pool[3]]

    ev_cw = None
    for cc in range(8):
        wsl3, wst3, wstw3 = ws_g.acquire()
        ev_l = ctx.dma("sync", wst3[:], cprojT_d[cc * 128:(cc + 1) * 128, :],
                       waits=wstw3, inc=dW)
        ev_cw = ctx.op("gpsimd", lambda e, i=wst3, cc=cc,
            o=cprojT_bf[:, cc * 1024:(cc + 1) * 1024]: e.tensor_copy(o, i[:]),
            waits=[ev_l] + p3_end, inc=sG)
        ws_g.release(wsl3, ev_cw)
    ctx.dma("scalar", cos4[:], cos4_d, inc=dC)
    ev_cs = ctx.dma("scalar", sin4[:], sin4_d, inc=dC)
    ev_cpb = ctx.dma("scalar", cprojb_sb[:], cprojb_d, inc=dC)
    ev_vones = ctx.op("vector", lambda e: e.memset(
        v_aug[:].rearrange("p (kc h j) -> p kc h j", kc=16, h=16)[:, :, :, 32:33], 1.0),
        writes=(v_aug,), inc=sV)

    p2_prev = []
    p_yh_rel = [[] for _ in range(4)]
    pe_defer = []
    catw_rel = {0: [], 1: []}

    def rec_cat_chunk(pi, dc, joff, par2, conv_list):
        j0 = pi * D + joff
        cwb2 = catw_bf.tiles[par2]
        wsl, wst, wstw = ws_g.acquire()
        first = (pi == 0 and dc == 0)
        ev_l1 = ctx.dma("sync", wst[:, 0:128],
                        catT[dc * 128:(dc + 1) * 128, j0:j0 + 128],
                        waits=wstw + (catw_rel[par2] if first else []), inc=dW)
        wsl2, wst2, wstw2 = ws_u.acquire()
        ev_l2 = ctx.dma("sync", wst2[:, 0:128],
                        catT[dc * 128:(dc + 1) * 128, j0 + 128:j0 + 256],
                        waits=wstw2 + (catw_rel[par2] if first else []), inc=dW)
        if first:
            catw_rel[par2] = []
        ctx.op("gpsimd", lambda e, i=wst[:, 0:128],
            o=cwb2[:, pi * FF + dc * 256: pi * FF + dc * 256 + 128]:
            e.tensor_copy(o, i), waits=[ev_l1], inc=sG)
        ev = ctx.op("gpsimd", lambda e, i=wst2[:, 0:128],
            o=cwb2[:, pi * FF + dc * 256 + 128: pi * FF + dc * 256 + 256]:
            e.tensor_copy(o, i), waits=[ev_l2], inc=sG)
        ws_g.release(wsl, ev)
        ws_u.release(wsl2, ev)
        conv_list.append(ev)

    catw_carry = None
    for s in range(B_PER_CORE):
        if s > 0:
            ctx.barrier(("tensor", "vector", "scalar"), p2_prev)
        es2y = ExitStack()
        y4 = [SB(es2y, f"y4_{s}_{t}", [128, T], BF16, side="right") for t in range(4)]
        es2a = ExitStack()
        q4 = [SB(es2a, f"q4_{s}_{t}", [128, T], BF16) for t in range(4)]
        k4 = [SB(es2a, f"k4_{s}_{t}", [128, T], BF16) for t in range(4)]

        seq_fill = []
        cat_chunks = [(pi, dc) for pi in range(3) for dc in range(8)]
        if catw_carry is not None:
            catw_conv = {0: catw_carry}
        else:
            catw_conv = {0: []}
            for pi, dc in cat_chunks:
                rec_cat_chunk(pi, dc, 256 * 0, 0, catw_conv[0])

        for tau in range(4):
            conv_cur_cat = catw_conv[tau]
            cwb = catw_bf.tiles[tau % 2]
            catw_conv[tau + 1] = []

            for tc in range(4):
                if tau + 1 < 4 and tc < 2:
                    for pi, dc in cat_chunks[tc * 12:(tc + 1) * 12]:
                        rec_cat_chunk(pi, dc, 256 * (tau + 1), (tau + 1) % 2,
                                      catw_conv[tau + 1])
                nsl, xnt, xnw = xn_sb.acquire()
                ev_xn = None
                for c in range(8):
                    ev_xn = ctx.dma("sync", xnt[:, c * 512:(c + 1) * 512],
                        xn_hbm[c, :, s * T + tc * 512: s * T + tc * 512 + 512],
                        waits=([xn_group_ev[(s * T + tc * 512) // 1024]] +
                               (xnw if c == 0 else [])),
                        inc=dN)

                ev_q4t = [None, None]
                for gg in range(2):
                    g = 2 * tau + gg
                    mmres = {}
                    for pi, part in enumerate(("q", "k", "v")):
                        qsl, pp, ppw = p_qkv.acquire()
                        for dc in range(8):
                            ctx.op("tensor", lambda e, pp=pp,
                                lh=cwb[:, pi * FF + dc * 256 + gg * 128:
                                        pi * FF + dc * 256 + gg * 128 + 128],
                                rh=xnt[:, dc * 512:(dc + 1) * 512],
                                st=(dc == 0), sp=(dc == 7): e.matmul(
                                pp[:], lh, rh, start=st, stop=sp),
                                waits=([conv_cur_cat[-1], ev_xn] if dc == 0 else []) +
                                      (ppw if dc == 0 else []),
                                inc=sT if dc == 7 else None)
                        mmres[part] = (qsl, pp, (sT, sT.n))
                    sbm = {}
                    for pi, part in enumerate(("q", "k", "v")):
                        qsl, pp, ev_mm = mmres[part]
                        ssl2, qsb, qsbw = qkv_sb.acquire()
                        col = {"q": g, "k": 8 + g, "v": 16 + g}[part]
                        ev_ev = ctx.op("scalar", lambda e, pp=pp, o=qsb,
                            b=catb_sb[:, col:col + 1]: e.activation(
                            o[:], pp[:], AF.Identity, bias=b),
                            reads=(pp, catb_sb, qsb), writes=(qsb,),
                            waits=[ev_mm, ld_done] + qsbw, inc=sS)
                        p_qkv.release(qsl, ev_ev)
                        sbm[part] = (qsb, ssl2, ev_ev)
                    latres = {}
                    for part in ("q", "k"):
                        qsb, qslot, ev_q = sbm[part]
                        wT = q2lT2_bf if part == "q" else kv2lT2_bf
                        lsl, pl, plw = p_lat.acquire()
                        ev_lat = ctx.op("tensor", lambda e, pl=pl, wT=wT, qsb=qsb:
                            e.matmul(pl[0:64, :], wT[:], qsb[:], start=True, stop=True),
                            waits=[ev_q, ev_smalls] + plw, inc=sT)
                        latres[part] = (lsl, pl, ev_lat, qslot)
                    qlres = {}
                    for part in ("q", "k"):
                        lsl, pl, ev_lat, qslot = latres[part]
                        bias = q2lb2 if part == "q" else kv2lb2
                        qlsl, qls, qlsw = ql_sb.acquire()
                        ev_ql = ctx.op("scalar", lambda e, pl=pl, o=qls, b=bias:
                            e.activation(o[:], pl[0:64, :], AF.Identity, bias=b[:]),
                            reads=(pl, bias, qls), writes=(qls,),
                            waits=[ev_lat, ld_done] + qlsw, inc=sS)
                        p_lat.release(lsl, ev_ql)
                        qkv_sb.release(qslot, ev_lat)
                        qlres[part] = (qlsl, qls, ev_ql)
                    rotres = {}
                    for part in ("q", "k"):
                        qlsl, qls, ev_ql = qlres[part]
                        rsl, pr, prw = p_lat.acquire()
                        ev_rot = ctx.op("tensor", lambda e, pr=pr, qls=qls: e.matmul(
                            pr[0:64, :], protT4[0:64, 0:64], qls[:],
                            start=True, stop=True),
                            waits=[ev_ql, ld_done] + prw, inc=sT)
                        rotres[part] = (rsl, pr, ev_rot, qlsl, qls, ev_ql)
                    vsb, vslot, ev_v = sbm["v"]
                    vres = []
                    for sub in range(4):
                        vsl, pv, pvw = p_qkv.acquire()
                        ctx.op("tensor", lambda e, pv=pv,
                            lh=vsb[:, sub * 128:(sub + 1) * 128]:
                            e.matmul(pv[:, 0:64], lh, kv2lT2_bf[:],
                                     start=True, stop=False),
                            waits=[ev_v, ev_smalls] + pvw)
                        ev_vb = ctx.op("tensor", lambda e, pv=pv: e.matmul(
                            pv[:, 0:64], ones_f[0:1, :], kv2lb2v[:],
                            start=False, stop=True),
                            waits=[ld_done], inc=sT)
                        vres.append((vsl, pv, ev_vb))
                    qkv_sb.release(vslot, vres[-1][2])
                    for part in ("q", "k"):
                        rsl, pr, ev_rot, qlsl, qls, ev_ql = rotres[part]
                        fsl, rpf, rpfw = rope_f.acquire()
                        ctx.op("vector", lambda e, qls=qls, o=rpf[:, 0:512],
                            c=cos4[0:64, tc * 512:(tc + 1) * 512]: e.tensor_tensor(
                            o, qls[:], c, ALU.mult),
                            reads=(qls, cos4, rpf), writes=(rpf,),
                            waits=[ev_ql, ev_cs] + rpfw)
                        ev_t2 = ctx.op("vector", lambda e, pr=pr, o=rpf[:, 512:1024],
                            sn=sin4[0:64, tc * 512:(tc + 1) * 512]: e.tensor_tensor(
                            o, pr[0:64, :], sn, ALU.mult),
                            reads=(pr, sin4, rpf), writes=(rpf,), waits=[ev_rot], inc=sV)
                        p_lat.release(rsl, ev_t2)
                        ql_sb.release(qlsl, ev_t2)
                        if part == "q":
                            if gg == 0:
                                qtsl, qt, qtw = q4tmp.acquire()
                            ev_ro = ctx.op("vector", lambda e, o=qt[64 * gg:64 * gg + 64, :],
                                a=rpf[:, 0:512], b=rpf[:, 512:1024]: e.tensor_tensor(
                                o, a, b, ALU.add),
                                reads=(rpf, qt), writes=(qt,),
                                waits=(qtw if gg == 0 else []), inc=sV)
                            ev_q4t[gg] = ev_ro
                        else:
                            ev_ro = ctx.op("vector", lambda e,
                                o=k4[tau][64 * gg:64 * gg + 64, tc * 512:(tc + 1) * 512],
                                a=rpf[:, 0:512], b=rpf[:, 512:1024]: e.tensor_tensor(
                                o, a, b, ALU.add),
                                reads=(rpf, k4[tau]), writes=(k4[tau],), inc=sV)
                            seq_fill.append(ev_ro)
                        rope_f.release(fsl, ev_ro)
                    for sub in range(4):
                        vsl, pv, ev_vb = vres[sub]
                        kc = tc * 4 + sub
                        ev_cp = None
                        for hh in range(2):
                            h2 = 2 * g + hh
                            ev_cp = ctx.op("scalar", lambda e,
                                o=v_aug[:, kc * 528 + h2 * 33: kc * 528 + h2 * 33 + 32],
                                i=pv[:, hh * 32:(hh + 1) * 32]: e.copy(o, i),
                                reads=(pv, v_aug), writes=(v_aug,),
                                waits=[ev_vb, ev_vones], inc=sS)
                        p_qkv.release(vsl, ev_cp)
                        seq_fill.append(ev_cp)

                wslb, pw, pww = p_s.acquire()
                ev_wqk = ctx.op("tensor", lambda e, pw=pw, qt=qt: e.matmul(
                    pw[:], wqkT4_bf[:], qt[:], start=True, stop=True),
                    waits=[ev_q4t[0], ev_q4t[1], ev_smalls] + pww, inc=sT)
                q4tmp.release(qtsl, ev_wqk)
                ev_q4 = ctx.op("scalar", lambda e, pw=pw,
                    o=q4[tau][:, tc * 512:(tc + 1) * 512]: e.activation(
                    o, pw[:], AF.Identity, bias=wqkb4[:]),
                    reads=(pw, wqkb4, q4[tau]), writes=(q4[tau],),
                    waits=[ev_wqk, ld_done], inc=sS)
                p_s.release(wslb, ev_q4)
                seq_fill.append(ev_q4)
                xn_sb.release(nsl, (sT, sT.n))
            catw_rel[tau % 2] = [(sT, sT.n)]

        first_score = True
        y4_evs = [[] for _ in range(4)]
        for h in range(H):
            tau, rb = h // 4, 32 * (h % 4)
            for qc in range(4):
                pyh_w, p_yh_rel[qc] = p_yh_rel[qc], []
                kcs = list(range(min(4 * qc + 4, 16)))
                pend = []
                ev_pv = None

                def emit_pv(entry, qc=qc):
                    kc, ex, esl, pv_wait, extra = entry
                    ev = ctx.op("tensor", lambda e, w=extra["w"], q0=extra["q0"],
                        lh=extra["lh"], rh=extra["rh"],
                        st=(kc == 0), sp=(kc == kcs[-1]): e.matmul(
                        p_yh[qc][0:33, q0 - 512 * qc: q0 - 512 * qc + w],
                        lh, rh, start=st, stop=sp),
                        waits=pv_wait + (pyh_w if kc == 0 else []), inc=sT)
                    exp_sb.release(esl, ev)
                    return ev

                for i, kc in enumerate(kcs):
                    q0 = max(512 * qc, 128 * kc)
                    w = 512 * qc + 512 - q0
                    psl, ps, psw = p_s.acquire()
                    ev_s = ctx.op("tensor", lambda e, ps=ps, w=w,
                        lh=k4[tau][rb:rb + 32, kc * 128:(kc + 1) * 128],
                        rh=q4[tau][rb:rb + 32, q0:q0 + w], rb=rb: e.matmul(
                        ps[:, 0:w], lh, rh, start=True, stop=True,
                        tile_position=(rb, 0)),
                        waits=(seq_fill if first_score else []) + psw, inc=sT)
                    first_score = False
                    if pe_defer:
                        pe_defer.pop(0)()
                    esl, ex, exw = exp_sb.acquire()
                    ev_e = ctx.op("scalar", lambda e, ps=ps, ex=ex, w=w: e.activation(
                        ex[:, 0:w], ps[:, 0:w], AF.Exp, scale=RSQL),
                        reads=(ps, ex), writes=(ex,), waits=[ev_s] + exw, inc=sS)
                    p_s.release(psl, ev_e)
                    pv_wait = [ev_e]
                    if q0 == 128 * kc:
                        ev_m = ctx.op("vector", lambda e, ex=ex: e.tensor_tensor(
                            ex[:, 0:128], ex[:, 0:128], maskdiag[:], ALU.mult),
                            reads=(ex, maskdiag), writes=(ex,),
                            waits=[ev_e, ld_done], inc=sV)
                        pv_wait = [ev_m]
                    rh_ap = ex[:, 0:w]
                    lh_ap = v_aug[:, kc * 528 + h * 33: kc * 528 + h * 33 + 33]
                    pend.append((kc, ex, esl, pv_wait,
                                 {"w": w, "q0": q0, "lh": lh_ap, "rh": rh_ap}))
                    if len(pend) > 2:
                        ev_pv = emit_pv(pend.pop(0))
                while pend:
                    ev_pv = emit_pv(pend.pop(0))

                rsl, rc, rcw = rcp_sb.acquire()
                ctx.op("scalar", lambda e, rc=rc, qc=qc: e.activation(
                    rc[32:33, :], p_yh[qc][32:33, :], AF.Ln),
                    reads=(p_yh[qc], rc), writes=(rc,), waits=[ev_pv] + rcw)
                ev_rc = ctx.op("scalar", lambda e, rc=rc: e.activation(
                    rc[32:33, :], rc[32:33, :], AF.Exp, scale=-1.0),
                    reads=(rc,), writes=(rc,), inc=sS)
                bsl, bc, bcw = bcast_sb.acquire()

                def emit_epi(qc=qc, tau=tau, rb=rb, rc=rc, rsl=rsl, bc=bc, bsl=bsl,
                             ev_rc=ev_rc, bcw=bcw):
                    ev_bc = ctx.op("tensor", lambda e, rc=rc, qc=qc: e.matmul(
                        p_yh[qc][64:96, :], ones_f[32:33, 0:32], rc[32:33, :],
                        start=True, stop=True, tile_position=(32, 64)),
                        waits=[ev_rc], inc=sT)
                    ev_bcs = ctx.op("scalar", lambda e, bc=bc, qc=qc: e.copy(
                        bc[:], p_yh[qc][64:96, :]),
                        reads=(p_yh[qc], bc), writes=(bc,),
                        waits=[ev_bc] + bcw, inc=sS)
                    rcp_sb.release(rsl, ev_bc)
                    ev_tt = ctx.op("vector", lambda e, qc=qc, bc=bc,
                        o=y4[tau][rb:rb + 32, qc * 512:(qc + 1) * 512]: e.tensor_tensor(
                        o, p_yh[qc][0:32, :], bc[:], ALU.mult),
                        reads=(p_yh[qc], bc, y4[tau]), writes=(y4[tau],),
                        waits=[ev_bcs], inc=sV)
                    bcast_sb.release(bsl, ev_tt)
                    p_yh_rel[qc].append(ev_tt)
                    y4_evs[tau].append(ev_tt)

                pe_defer.append(emit_epi)
        while pe_defer:
            pe_defer.pop(0)()

        es2a.close()
        es2b = ExitStack()
        y_fm = SB(es2b, f"y_fm_{s}", [128, 8 * T], BF16)

        yfm_evs = []
        for tau in range(4):
            for pp_i in range(2):
                for tc in range(4):
                    osl, po, pow_ = p_s.acquire()
                    ev_mm = ctx.op("tensor", lambda e, po=po, pp_i=pp_i,
                        lh=l2oT2_bf[64 * pp_i:64 * pp_i + 64, :],
                        rh=y4[tau][64 * pp_i:64 * pp_i + 64, tc * 512:(tc + 1) * 512]:
                        e.matmul(po[:], lh, rh, start=True, stop=True,
                                 tile_position=(64 * pp_i, 0)),
                        waits=y4_evs[tau] + pow_ + [ev_smalls], inc=sT)
                    cc = 2 * tau + pp_i
                    ev_ev = ctx.op("scalar", lambda e, po=po,
                        o=y_fm[:, cc * T + tc * 512: cc * T + tc * 512 + 512]:
                        e.activation(o, po[:], AF.Identity, bias=l2ob2[:]),
                        reads=(po, l2ob2, y_fm), writes=(y_fm,),
                        waits=[ev_mm, ld_done], inc=sS)
                    p_s.release(osl, ev_ev)
                    yfm_evs.append(ev_ev)

        es2y.close()

        po_seed = (sS, sS.n)
        p_o = RBuf([(pool[4], pool[5]), (pool[6], pool[7])])
        p_o.seed_all([po_seed])
        for tl in range(16):
            ti = s * 16 + tl
            asl, ai, aiw = acc_io.acquire()
            ev_ai = ctx.dma("scalar", ai[:], attn_acc[ti * 128:(ti + 1) * 128, :],
                            waits=[acc_tile_ev[ti]] + aiw, inc=dV)
            osl2, (poh0, poh1), pow_ = p_o.acquire()
            for half, ph in ((0, poh0), (1, poh1)):
                for cc in range(8):
                    ctx.op("tensor", lambda e, ph=ph,
                        lh=y_fm[:, cc * T + tl * 128: cc * T + tl * 128 + 128],
                        rh=cprojT_bf[:, cc * 1024 + half * 512: cc * 1024 + half * 512 + 512],
                        st=(cc == 0): e.matmul(ph[:], lh, rh, start=st, stop=False),
                        waits=(yfm_evs + [ev_cw] + pow_) if cc == 0 and half == 0 else [])
                ctx.op("tensor", lambda e, ph=ph,
                    rh=cprojb_sb[:, half * 512:(half + 1) * 512]: e.matmul(
                    ph[:], ones_f[0:1, :], rh, start=False, stop=True),
                    waits=[ev_cpb, ld_done], inc=sT if half == 1 else None)
            ev_bias = (sT, sT.n)
            osl3, oo2, oow2 = acc_io.acquire()
            ev_a0 = ctx.op("vector", lambda e, ph=poh0, oo=oo2, ai=ai: e.tensor_tensor(
                oo[:, 0:512], ph[:], ai[:, 0:512], ALU.add),
                reads=(poh0, ai, oo2), writes=(oo2,),
                waits=[ev_bias, ev_ai] + oow2, inc=sV)
            ev_a1 = ctx.op("vector", lambda e, ph=poh1, oo=oo2, ai=ai: e.tensor_tensor(
                oo[:, 512:1024], ph[:], ai[:, 512:1024], ALU.add),
                reads=(poh1, ai, oo2), writes=(oo2,), inc=sV)
            p_o.release(osl2, ev_a1)
            acc_io.release(asl, ev_a1)
            so, to = divmod(ti * 128, T)
            ev_st = ctx.dma("gpsimd", out_d[so, to:to + 128, :], oo2[:],
                            reads=(oo2,), waits=[ev_a0, ev_a1], inc=stO)
            acc_io.release(osl3, ev_st)

        if s + 1 < B_PER_CORE:
            catw_carry = []
            for pi, dc in cat_chunks:
                rec_cat_chunk(pi, dc, 0, 0, catw_carry)
        p2_prev = [(sV, sV.n), (sS, sS.n), (sT, sT.n), (sG, sG.n)]
        es2b.close()

    ctx.wait_only("gpsimd", [(stO, stO.n)])
    es2.close()

    with nc.Block() as block:
        @block.sync
        def _(h):
            ctx.replay("sync", h)

        @block.tensor
        def _(h):
            ctx.replay("tensor", h)

        @block.vector
        def _(h):
            ctx.replay("vector", h)

        @block.scalar
        def _(h):
            ctx.replay("scalar", h)

        @block.gpsimd
        def _(h):
            ctx.replay("gpsimd", h)

    es.close()
    return nc


def make_host_inputs(L0_attn_c_attn_w, L0_attn_c_attn_b, L0_attn_q2l_w,
                     L0_attn_q2l_b, L0_attn_kv2l_w, L0_attn_kv2l_b,
                     L0_attn_l2o_w, L0_attn_l2o_b, L0_attn_wqk_w, L0_attn_wqk_b,
                     L0_attn_cproj_w, L0_attn_cproj_b, L0_ffn_router_kernel_DE,
                     L0_ffn_kernel_gating_EDF, L0_ffn_kernel_up_EDF,
                     L0_ffn_kernel_down_EFD):
    f32 = np.float32
    bf16 = ml_dtypes.bfloat16
    C = lambda a: np.ascontiguousarray(np.asarray(a), dtype=f32)

    inv_freq = (1.0 / (f32(10000.0) ** (np.arange(0, L, 2, dtype=f32) / f32(L)))).astype(f32)
    t = np.arange(T, dtype=f32)
    freqs = np.outer(t, inv_freq).astype(f32)
    cos = np.repeat(np.cos(freqs).astype(f32), 2, axis=-1)[:, :L]
    sin = np.repeat(np.sin(freqs).astype(f32), 2, axis=-1)[:, :L]

    P = np.zeros((L, L), dtype=f32)
    for i in range(L // 2):
        P[2 * i, 2 * i + 1] = -1.0
        P[2 * i + 1, 2 * i] = 1.0

    q2lT = C(np.asarray(L0_attn_q2l_w).T)
    kv2lT = C(np.asarray(L0_attn_kv2l_w).T)
    l2oT = C(np.asarray(L0_attn_l2o_w).T)
    wqkT = C(np.asarray(L0_attn_wqk_w).T)
    l2oT2_blk = _blkdiag([l2oT, l2oT])

    return {
        "catT": C(np.asarray(L0_attn_c_attn_w).T),
        "catb_fm": C(np.asarray(L0_attn_c_attn_b).reshape(24, 128).T),
        "q2lT2": _blkdiag([q2lT, q2lT]),
        "kv2lT2": _blkdiag([kv2lT, kv2lT]),
        "q2lb2": C(np.tile(np.asarray(L0_attn_q2l_b), 2)[:, None]),
        "kv2lb2": C(np.tile(np.asarray(L0_attn_kv2l_b), 2)[:, None]),
        "kv2lb2v": C(np.tile(np.asarray(L0_attn_kv2l_b), 2)[None, :]),
        "wqkT4": _blkdiag([wqkT] * 4),
        "wqkb4": C(np.tile(np.asarray(L0_attn_wqk_b), 4)[:, None]),
        "protT4": _blkdiag([C(P.T)] * 4),
        "l2oT2": C(np.concatenate([l2oT2_blk, l2oT2_blk], axis=0)),
        "l2ob2": C(np.tile(np.asarray(L0_attn_l2o_b), 2)[:, None]),
        "cprojT": C(np.asarray(L0_attn_cproj_w).T),
        "cprojb": C(np.asarray(L0_attn_cproj_b)[None, :]),
        "router_w": C(L0_ffn_router_kernel_DE),
        "wg": C(L0_ffn_kernel_gating_EDF),
        "wu": C(L0_ffn_kernel_up_EDF),
        "wd": C(L0_ffn_kernel_down_EFD),
        "cos4": np.tile(C(cos.T), (4, 1)),
        "sin4": np.tile(C(sin.T), (4, 1)),
        
        "ident_f": np.eye(128, dtype=f32),
        "maskdiag": np.triu(np.ones((128, 128), dtype=f32)),
    }


def kernel(L0_attn_x, **w):
    nc = build_program()
    shared = make_host_inputs(**w)
    xs = np.asarray(L0_attn_x)
    in_maps = []
    for c in range(8):
        m = dict(shared)
        m["x"] = np.ascontiguousarray(xs[2 * c:2 * c + 2])
        in_maps.append(m)
    res = run_bass_kernel_spmd(nc, in_maps, core_ids=list(range(8)))
    return np.concatenate([res.results[c]["out"] for c in range(8)], axis=0)


# version 06
# speedup vs baseline: 1.0848x; 1.0848x over previous
import math
import numpy as np
import ml_dtypes
from contextlib import ExitStack

import concourse.bass as bass
import concourse.mybir as mybir
from concourse.bass_utils import run_bass_kernel_spmd

F32 = mybir.dt.float32
BF16 = mybir.dt.bfloat16
AF = mybir.ActivationFunctionType
ALU = mybir.AluOpType
AX = mybir.AxisListType.X

B_PER_CORE = 2
T = 2048
D = 1024
H = 16
L = 32
E = 8
FF = 2048
NTOK = B_PER_CORE * T
NTILE = NTOK // 128
RSQL = 1.0 / math.sqrt(L)

ENGINES = ("tensor", "vector", "scalar", "gpsimd", "sync")


class Sem:
    __slots__ = ("h", "n", "name")

    def __init__(self, h, name):
        self.h = h
        self.n = 0
        self.name = name


class Ctx:

    def __init__(self, nc):
        self.nc = nc
        self.q = {e: [] for e in ENGINES}
        self.waited = {e: {} for e in ENGINES}
        self.rsince = {e: set() for e in ENGINES}
        self.wsince = {e: set() for e in ENGINES}
        self.counts = {e: 0 for e in ENGINES}

    def op(self, eng, build, reads=(), writes=(), waits=(), inc=None, inc_by=1,
           is_dma=False):
        q = self.q[eng]
        wmax = {}
        for ev in waits:
            if ev is None:
                continue
            sem, val = ev
            if val > wmax.get(sem, 0):
                wmax[sem] = val
        for sem, val in wmax.items():
            if self.waited[eng].get(sem, 0) >= val:
                continue
            self.waited[eng][sem] = val
            q.append(("wait", sem, val))
        rk = {id(b) for b in reads}
        wk = {id(b) for b in writes}
        rs, ws = self.rsince[eng], self.wsince[eng]
        if (rk & ws) or (wk & ws) or (wk & rs):
            q.append(("drain",))
            rs.clear()
            ws.clear()
        if not is_dma:
            rs |= rk
            ws |= wk
        self.counts[eng] += 1
        if inc is not None:
            inc.n += inc_by
            q.append(("op", build, inc.h, inc_by))
            return (inc, inc.n)
        q.append(("op", build, None, 0))
        return None

    def dma(self, eng, out, in_, reads=(), waits=(), inc=None):
        return self.op(eng, lambda e, o=out, i=in_: e.dma_start(out=o, in_=i),
                       reads=reads, waits=waits, inc=inc, inc_by=16, is_dma=True)

    def wait_only(self, eng, waits):
        wmax = {}
        for ev in waits:
            if ev is None:
                continue
            sem, val = ev
            if val > wmax.get(sem, 0):
                wmax[sem] = val
        for sem, val in wmax.items():
            if self.waited[eng].get(sem, 0) >= val:
                continue
            self.waited[eng][sem] = val
            self.q[eng].append(("wait", sem, val))

    def barrier(self, engines, evs):
        for e in engines:
            self.wait_only(e, evs)

    def replay(self, eng, handle):
        for item in self.q[eng]:
            k = item[0]
            if k == "wait":
                handle.wait_ge(item[1].h, item[2])
            elif k == "drain":
                handle.drain()
            else:
                _, build, sem_h, inc_by = item
                ins = build(handle)
                if sem_h is not None:
                    ins.then_inc(sem_h, inc_by)


class RBuf:

    def __init__(self, tiles):
        self.tiles = list(tiles)
        self.rel = [[] for _ in self.tiles]
        self._next = 0

    def acquire(self):
        s = self._next
        self._next = (self._next + 1) % len(self.tiles)
        w = self.rel[s]
        self.rel[s] = []
        return s, self.tiles[s], w

    def release(self, s, ev):
        if ev is not None:
            self.rel[s].append(ev)

    def seed_all(self, evs):
        for s in range(len(self.tiles)):
            self.rel[s].extend(e for e in evs if e is not None)


def _blkdiag(blocks):
    n = sum(b.shape[0] for b in blocks)
    m = sum(b.shape[1] for b in blocks)
    out = np.zeros((n, m), dtype=blocks[0].dtype)
    r = c = 0
    for b in blocks:
        out[r:r + b.shape[0], c:c + b.shape[1]] = b
        r += b.shape[0]
        c += b.shape[1]
    return out


def build_program(debug=()):
    nc = bass.Bass("TRN2", target_bir_lowering=False)
    es = ExitStack()

    def SB(st, name, shp, dt, side=None):
        return st.enter_context(nc.sbuf_tensor(name, shp, dt, side=side))

    def PS(st, name, shp, dt=F32):
        return st.enter_context(nc.psum_tensor(name, shp, dt))

    dbg = lambda name: "ExternalOutput" if name in debug else "Internal"

    x = nc.dram_tensor("x", [B_PER_CORE, T, D], F32, kind="ExternalInput").ap()
    catT = nc.dram_tensor("catT", [D, 3 * D], F32, kind="ExternalInput").ap()
    catb_fm = nc.dram_tensor("catb_fm", [128, 24], F32, kind="ExternalInput").ap()
    q2lT2_d = nc.dram_tensor("q2lT2", [128, 64], F32, kind="ExternalInput").ap()
    kv2lT2_d = nc.dram_tensor("kv2lT2", [128, 64], F32, kind="ExternalInput").ap()
    q2lb2_d = nc.dram_tensor("q2lb2", [64, 1], F32, kind="ExternalInput").ap()
    kv2lb2_d = nc.dram_tensor("kv2lb2", [64, 1], F32, kind="ExternalInput").ap()
    kv2lb2v_d = nc.dram_tensor("kv2lb2v", [1, 64], F32, kind="ExternalInput").ap()
    wqkT4_d = nc.dram_tensor("wqkT4", [128, 128], F32, kind="ExternalInput").ap()
    wqkb4_d = nc.dram_tensor("wqkb4", [128, 1], F32, kind="ExternalInput").ap()
    protT4_d = nc.dram_tensor("protT4", [128, 128], F32, kind="ExternalInput").ap()
    l2oT2_d = nc.dram_tensor("l2oT2", [128, 128], F32, kind="ExternalInput").ap()
    l2ob2_d = nc.dram_tensor("l2ob2", [128, 1], F32, kind="ExternalInput").ap()
    cprojT_d = nc.dram_tensor("cprojT", [D, D], F32, kind="ExternalInput").ap()
    cprojb_d = nc.dram_tensor("cprojb", [1, D], F32, kind="ExternalInput").ap()
    router_d = nc.dram_tensor("router_w", [D, E], F32, kind="ExternalInput").ap()
    wg_d = nc.dram_tensor("wg", [E, D, FF], F32, kind="ExternalInput").ap()
    wu_d = nc.dram_tensor("wu", [E, D, FF], F32, kind="ExternalInput").ap()
    wd_d = nc.dram_tensor("wd", [E, FF, D], F32, kind="ExternalInput").ap()
    cos4_d = nc.dram_tensor("cos4", [128, T], F32, kind="ExternalInput").ap()
    sin4_d = nc.dram_tensor("sin4", [128, T], F32, kind="ExternalInput").ap()
    
    ident_f_d = nc.dram_tensor("ident_f", [128, 128], F32, kind="ExternalInput").ap()
    maskdiag_d = nc.dram_tensor("maskdiag", [128, 128], F32, kind="ExternalInput").ap()

    xn_hbm = nc.dram_tensor("xn_hbm", [8, 128, NTOK], BF16, kind=dbg("xn_hbm")).ap()
    attn_acc = nc.dram_tensor("attn_acc", [NTOK, D], F32, kind=dbg("attn_acc")).ap()
    out_d = nc.dram_tensor("out", [B_PER_CORE, T, D], F32, kind="ExternalOutput").ap()
    fullw_dbg = None
    if "fullw" in debug:
        fullw_dbg = nc.dram_tensor("fullw", [128, NTILE * 8], F32,
                                   kind="ExternalOutput").ap()

    S = {n: Sem(es.enter_context(nc.semaphore(n)), n)
         for n in ("dX", "dN", "dW", "dA", "dC", "dD", "dV",
                   "sT", "sV", "sS", "sG", "stO")}
    dX, dN, dW, dA, dC = S["dX"], S["dN"], S["dW"], S["dA"], S["dC"]
    dD, dV = S["dD"], S["dV"]
    sT, sV, sS, sG, stO = S["sT"], S["sV"], S["sS"], S["sG"], S["stO"]

    ctx = Ctx(nc)

    fullw_all = SB(es, "fullw_all", [128, NTILE * 8], F32)
    ident_bf = SB(es, "ident_bf_sb", [128, 128], BF16)
    ident_f = SB(es, "ident_f_sb", [128, 128], F32)
    ones_f = SB(es, "ones_f", [128, 128], F32)
    maskdiag = SB(es, "maskdiag_sb", [128, 128], BF16)
    catb_sb = SB(es, "catb_sb", [128, 24], F32)
    q2lT2_bf = SB(es, "q2lT2_bf", [128, 64], BF16)
    kv2lT2_bf = SB(es, "kv2lT2_bf", [128, 64], BF16)
    q2lb2 = SB(es, "q2lb2_sb", [64, 1], F32)
    kv2lb2 = SB(es, "kv2lb2_sb", [64, 1], F32)
    kv2lb2v = SB(es, "kv2lb2v_sb", [1, 64], F32)
    wqkT4_bf = SB(es, "wqkT4_bf", [128, 128], BF16)
    wqkb4 = SB(es, "wqkb4_sb", [128, 1], F32)
    protT4 = SB(es, "protT4_bf", [128, 128], BF16)
    l2oT2_bf = SB(es, "l2oT2_bf", [128, 128], BF16)
    l2ob2 = SB(es, "l2ob2_sb", [128, 1], F32)
    router_sb = SB(es, "router_sb", [128, 8 * E], F32)
    xn_sb = RBuf([SB(es, f"xn_sb{i}", [128, 8 * 512], BF16) for i in range(2)])
    ws_g = RBuf([SB(es, f"ws_g{i}", [128, 1024], F32) for i in range(2)])
    ws_u = RBuf([SB(es, f"ws_u{i}", [128, 1024], F32) for i in range(2)])
    ws_d = RBuf([SB(es, f"ws_d{i}", [128, 1024], F32) for i in range(2)])
    acc_io = RBuf([SB(es, f"acc_io{i}", [128, D], F32) for i in range(4)])
    catw_bf = RBuf([SB(es, f"catw_bf{i}", [128, 3 * FF], BF16) for i in range(2)])

    es1 = ExitStack()
    smalls_f = SB(es1, "smalls_f", [128, 640], F32)
    xin = RBuf([SB(es1, f"xin{i}", [128, D], F32) for i in range(2)])
    for dst, src in ((ident_f, ident_f_d), (catb_sb, catb_fm),
                     (q2lb2, q2lb2_d), (kv2lb2, kv2lb2_d), (kv2lb2v, kv2lb2v_d),
                     (wqkb4, wqkb4_d), (l2ob2, l2ob2_d)):
        ctx.dma("sync", dst[:], src, inc=dW)
    for dc in range(8):
        ctx.dma("sync", router_sb[:, dc * 8:(dc + 1) * 8],
                router_d[dc * 128:(dc + 1) * 128, :], inc=dW)
    sm_off = {}
    off = 0
    for name, src, w in (("q2lT2", q2lT2_d, 64), ("kv2lT2", kv2lT2_d, 64),
                         ("wqkT4", wqkT4_d, 128), ("l2oT2", l2oT2_d, 128),
                         ("protT4", protT4_d, 128), ("maskdiag", maskdiag_d, 128)):
        sm_off[name] = (off, w)
        ctx.dma("sync", smalls_f[:, off:off + w], src, inc=dW)
        off += w
    ld_done = (dW, dW.n)

    ctx.op("vector", lambda e: e.memset(ones_f[:], 1.0), writes=(ones_f,))
    ev_smalls = None
    ctx.op("vector", lambda e: e.tensor_copy(ident_bf[:], ident_f[:]),
           reads=(ident_f,), writes=(ident_bf,), waits=[ld_done])
    for name, dst in (("q2lT2", q2lT2_bf), ("kv2lT2", kv2lT2_bf),
                      ("wqkT4", wqkT4_bf), ("l2oT2", l2oT2_bf),
                      ("protT4", protT4), ("maskdiag", maskdiag)):
        o, w = sm_off[name]
        ev_smalls = ctx.op("vector",
            lambda e, d=dst, src=smalls_f[:, o:o + w]: e.tensor_copy(d[:], src),
            reads=(smalls_f,), writes=(dst,), waits=[ld_done], inc=sV)

    xn32 = RBuf([SB(es1, f"xn32_{i}", [128, D], F32) for i in range(2)])
    xnbf = RBuf([SB(es1, f"xnbf_{i}", [128, D], BF16) for i in range(2)])
    xnfm32 = RBuf([SB(es1, f"xnfm32_{i}", [128, D], F32) for i in range(2)])
    statsb = RBuf([SB(es1, f"stats_{i}", [128, 8], F32) for i in range(2)])
    acc8 = SB(es1, "xnfm_acc", [128, 8 * 1024], BF16)
    r8b = RBuf([SB(es1, f"r8_{i}", [128, 8], F32) for i in range(2)])
    rtmp = SB(es1, "rtmp", [128, 48], F32)
    p_trb = PS(es1, "p_trb", [128, 1024], BF16)
    p_trf = PS(es1, "p_trf", [128, 1024], F32)
    p_log = RBuf([PS(es1, f"p_log{i}", [128, 8]) for i in range(2)])

    acc8_rel = []
    p1_defer = []
    xn_group_ev = [None] * (NTILE // 8)
    fullw_ready = [None] * NTILE
    trb_rel = None
    trf_rel = None

    for ti in range(NTILE):
        s0, t0 = divmod(ti * 128, T)
        xsl, xt, xw = xin.acquire()
        ev_x = ctx.dma("sync", xt[:], x[s0, t0:t0 + 128, :], waits=xw, inc=dX)

        ssl, st_, stw = statsb.acquire()
        bsl, xbf, xbfw = xnbf.acquire()
        ctx.op("vector", lambda e, d=st_[:, 1:2]: e.memset(d, 0.0),
               reads=(), writes=(st_,), waits=stw)
        ctx.op("vector", lambda e, xt=xt, xb=xbf, d=st_[:, 1:2]:
               e.scalar_tensor_tensor(xb[:], xt[:], 0.0, xt[:], ALU.add, ALU.mult,
                                      accum_out=d),
               reads=(xt, st_), writes=(xbf, st_), waits=[ev_x] + xbfw)
        ctx.op("vector", lambda e, xt=xt, d=st_[:, 0:1]: e.reduce_sum(d, xt[:], axis=AX),
               reads=(xt,), writes=(st_,))
        ctx.op("vector", lambda e, s=st_[:, 0:1], d=st_[:, 2:3]:
               e.tensor_scalar(d, s, -1.0 / D, None, ALU.mult),
               reads=(st_,), writes=(st_,))
        ctx.op("vector", lambda e, s=st_[:, 2:3], d=st_[:, 3:4]:
               e.tensor_tensor(d, s, s, ALU.mult),
               reads=(st_,), writes=(st_,))
        ctx.op("vector", lambda e, s=st_[:, 1:2], d=st_[:, 4:5]:
               e.tensor_scalar(d, s, 1.0 / D, 1e-5, ALU.mult, ALU.add),
               reads=(st_,), writes=(st_,))
        ev_var = ctx.op("vector", lambda e, a=st_[:, 4:5], b=st_[:, 3:4]:
               e.tensor_tensor(a, a, b, ALU.subtract),
               reads=(st_,), writes=(st_,), inc=sV)
        ctx.op("scalar", lambda e, i=st_[:, 4:5], o=st_[:, 5:6]:
               e.activation(o, i, AF.Ln),
               reads=(st_,), writes=(st_,), waits=[ev_var])
        ev_rstd = ctx.op("scalar", lambda e, i=st_[:, 5:6], o=st_[:, 6:7]:
               e.activation(o, i, AF.Exp, scale=-0.5),
               reads=(st_,), writes=(st_,), inc=sS)

        nsl, x32, x32w = xn32.acquire()
        ev_xn32 = ctx.op("vector", lambda e, xt=xt, o=x32, m=st_[:, 2:3], r=st_[:, 6:7]:
               e.tensor_scalar(o[:], xt[:], m, r, ALU.add, ALU.mult),
               reads=(xt, st_), writes=(x32,), waits=[ev_rstd] + x32w, inc=sV)
        ev_xnbf = ctx.op("vector", lambda e, i=x32, o=xbf: e.tensor_copy(o[:], i[:]),
               reads=(x32,), writes=(xbf,), inc=sV)

        for c in range(8):
            ctx.op("tensor", lambda e, i=xbf[:, c * 128:(c + 1) * 128],
                   o=p_trb[:, c * 128:(c + 1) * 128]: e.transpose(o, i, ident_bf[:]),
                   waits=[ev_xnbf, ld_done] + ([trb_rel] if c == 0 else []),
                   inc=sT if c == 7 else None)
        ev_trb = (sT, sT.n)
        for c in range(8):
            ctx.op("tensor", lambda e, i=x32[:, c * 128:(c + 1) * 128],
                   o=p_trf[:, c * 128:(c + 1) * 128]: e.transpose(o, i, ident_f[:]),
                   waits=[ev_xn32] + ([trf_rel] if c == 0 else []),
                   inc=sT if c == 7 else None)
        ev_trf = (sT, sT.n)
        xnbf.release(bsl, ev_trb)
        xn32.release(nsl, ev_trf)

        g8 = ti % 8
        aw = []
        if g8 == 0:
            aw, acc8_rel = acc8_rel, []
        ev_accw = ctx.op("vector",
            lambda e, o=acc8[:].rearrange("p (c t) -> p c t", c=8)[:, :, g8 * 128:(g8 + 1) * 128],
            i=p_trb[:].rearrange("p (c t) -> p c t", c=8): e.tensor_copy(o, i),
            reads=(p_trb,), writes=(acc8,), waits=[ev_trb] + aw, inc=sV)
        trb_rel = ev_accw

        fsl, xf32, xf32w = xnfm32.acquire()
        ev_xf32 = ctx.op("scalar", lambda e, o=xf32: e.copy(o[:], p_trf[:]),
               reads=(p_trf,), writes=(xf32,), waits=[ev_trf] + xf32w, inc=sS)
        trf_rel = ev_xf32

        if g8 == 7:
            gi = ti // 8
            ev_g = ctx.dma("gpsimd",
                xn_hbm[:, :, gi * 1024:(gi + 1) * 1024].transpose([1, 0, 2]),
                acc8[:].rearrange("p (c t) -> p c t", c=8),
                waits=[ev_accw], inc=dN)
            xn_group_ev[gi] = ev_g
            acc8_rel.append(ev_g)

        lsl, plog, plogw = p_log.acquire()
        for dc in range(8):
            ctx.op("tensor", lambda e, pl=plog, lh=xf32[:, dc * 128:(dc + 1) * 128],
                   rh=router_sb[:, dc * 8:(dc + 1) * 8], st=(dc == 0), sp=(dc == 7):
                   e.matmul(pl[:], lh, rh, start=st, stop=sp),
                   waits=[ev_xf32, ld_done] + (plogw if dc == 0 else []),
                   inc=sT if dc == 7 else None)
        ev_logit = (sT, sT.n)
        xnfm32.release(fsl, ev_logit)

        def p1_epi(ti=ti, plog=plog, lsl=lsl, ev_logit=ev_logit):
            r8sl, r8, r8w = r8b.acquire()
            ro = (ti % 2) * 24
            c0 = rtmp[:, ro + 0:ro + 1]
            c1 = rtmp[:, ro + 1:ro + 2]
            c2 = rtmp[:, ro + 2:ro + 3]
            c3 = rtmp[:, ro + 3:ro + 4]
            c4 = rtmp[:, ro + 4:ro + 5]
            c5 = rtmp[:, ro + 5:ro + 6]
            v8 = rtmp[:, ro + 8:ro + 16]
            a8 = rtmp[:, ro + 16:ro + 24]
            ev1 = ctx.op("vector", lambda e, pl=plog, d=c0: e.reduce_max(d, pl[:], axis=AX,
                   negate=True),
                   reads=(plog, rtmp), writes=(rtmp,), waits=[ev_logit] + r8w, inc=sV)
            ev_p = ctx.op("scalar", lambda e, pl=plog, r8=r8, b=c0: e.activation(
                   r8[:], pl[:], AF.Exp, bias=b),
                   reads=(plog, rtmp, r8), writes=(r8,), waits=[ev1], inc=sS)
            p_log.release(lsl, ev_p)
            ctx.op("vector", lambda e, r8=r8, d=c1: e.reduce_sum(d, r8[:], axis=AX),
                   reads=(r8, rtmp), writes=(rtmp,), waits=[ev_p])
            ctx.op("vector", lambda e, d=c1: e.reciprocal(d, d),
                   reads=(rtmp,), writes=(rtmp,))
            ctx.op("vector", lambda e, r8=r8, s=c1: e.tensor_scalar(
                   r8[:], r8[:], s, None, ALU.mult),
                   reads=(r8, rtmp), writes=(r8,))
            ctx.op("vector", lambda e, pl=plog, d=v8, s=c0: e.tensor_scalar(
                   d, pl[:], s, None, ALU.add),
                   reads=(plog, rtmp), writes=(rtmp,))
            ctx.op("vector", lambda e, d=a8, s=v8: e.tensor_scalar(
                   d, s, 0.0, None, ALU.is_equal),
                   reads=(rtmp,), writes=(rtmp,))
            ctx.op("vector", lambda e, pl=plog, d=v8, i1=a8: e.scalar_tensor_tensor(
                   d, i1, -1e30, pl[:], ALU.mult, ALU.add),
                   reads=(rtmp, plog), writes=(rtmp,))
            ctx.op("vector", lambda e, d=c2, s=v8: e.reduce_max(d, s, axis=AX, negate=True),
                   reads=(rtmp,), writes=(rtmp,))
            ctx.op("vector", lambda e, d=v8, s=c2: e.tensor_scalar(d, d, s, None, ALU.add),
                   reads=(rtmp,), writes=(rtmp,))
            ctx.op("vector", lambda e, d=v8: e.tensor_scalar(d, d, 0.0, None, ALU.is_equal),
                   reads=(rtmp,), writes=(rtmp,))
            ctx.op("vector", lambda e, r8=r8, i1=a8: e.tensor_tensor(
                   i1, i1, r8[:], ALU.mult),
                   reads=(rtmp, r8), writes=(rtmp,))
            ctx.op("vector", lambda e, s=a8, d=c3: e.reduce_sum(d, s, axis=AX),
                   reads=(rtmp,), writes=(rtmp,))
            ctx.op("vector", lambda e, r8=r8, i2=v8: e.tensor_tensor(
                   r8[:], r8[:], i2, ALU.mult),
                   reads=(rtmp, r8), writes=(r8,))
            ctx.op("vector", lambda e, r8=r8, d=c4: e.reduce_sum(d, r8[:], axis=AX),
                   reads=(r8, rtmp), writes=(rtmp,))
            ev_d = ctx.op("vector", lambda e, a=c4, b=c3: e.tensor_tensor(
                   a, a, b, ALU.subtract),
                   reads=(rtmp,), writes=(rtmp,), inc=sV)
            ev_e2 = ctx.op("scalar", lambda e, i=c4, o=c5: e.activation(o, i, AF.Exp),
                   reads=(rtmp,), writes=(rtmp,), waits=[ev_d], inc=sS)
            ctx.op("vector", lambda e, i=c5, o=c4: e.tensor_scalar(o, i, 1.0, None, ALU.add),
                   reads=(rtmp,), writes=(rtmp,), waits=[ev_e2])
            ctx.op("vector", lambda e, d=c4: e.reciprocal(d, d),
                   reads=(rtmp,), writes=(rtmp,))
            ctx.op("vector", lambda e, i=c4, o=c5: e.tensor_scalar(
                   o, i, -1.0, 1.0, ALU.mult, ALU.add),
                   reads=(rtmp,), writes=(rtmp,))
            ctx.op("vector", lambda e, i=c3, o=c2: e.reciprocal(o, i),
                   reads=(rtmp,), writes=(rtmp,))
            ctx.op("vector", lambda e, a=c2, b=c4: e.tensor_tensor(a, a, b, ALU.mult),
                   reads=(rtmp,), writes=(rtmp,))
            ctx.op("vector", lambda e, pi1=a8, s=c2,
                   d=fullw_all[:, ti * 8:(ti + 1) * 8]: e.tensor_scalar(
                   d, pi1, s, None, ALU.mult),
                   reads=(rtmp,), writes=(fullw_all,))
            ev_fw = ctx.op("vector", lambda e, i2=v8, nw2=c5,
                   d=fullw_all[:, ti * 8:(ti + 1) * 8]: e.scalar_tensor_tensor(
                   d, i2, nw2, d, ALU.mult, ALU.add),
                   reads=(rtmp, fullw_all), writes=(fullw_all,), inc=sV)
            fullw_ready[ti] = ev_fw
            r8b.release(r8sl, ev_fw)
        p1_defer.append(p1_epi)
        if len(p1_defer) > 1:
            p1_defer.pop(0)()
        statsb.release(ssl, ev_xn32)
        xin.release(xsl, ev_xn32)

    while p1_defer:
        p1_defer.pop(0)()

    if fullw_dbg is not None:
        ctx.dma("gpsimd", fullw_dbg, fullw_all[:], waits=list(fullw_ready), inc=stO)

    p1_end = [(sV, sV.n), (sS, sS.n), (sT, sT.n), (sG, sG.n)]
    es1.close()
    es3 = ExitStack()
    wg_bf = [SB(es3, f"wg_bf{p}", [128, 8 * 1024], BF16) for p in range(2)]
    wu_bf = [SB(es3, f"wu_bf{p}", [128, 8 * 1024], BF16) for p in range(2)]
    wd_bf = [SB(es3, f"wd_bf{p}", [128, 8 * 1024], BF16) for p in range(2)]
    fuse_bf2 = [SB(es3, f"fuse_bf{i}", [128, 8 * 512], BF16) for i in range(2)]
    silu_sb = RBuf([SB(es3, f"silu_sb{i}", [128, 512], BF16) for i in range(2)])
    out_sb = RBuf([SB(es3, f"out_sb{i}", [128, D], F32) for i in range(2)])

    pool3 = [PS(es3, f"p3bank{i}", [128, 512]) for i in range(6)]
    p_g = RBuf([pool3[0], pool3[1]])
    p_u = RBuf([pool3[2], pool3[3]])
    p_d = RBuf([pool3[4], pool3[5]])
    p1_psum_readers = [(sV, sV.n), (sS, sS.n)]
    p_g.seed_all(p1_psum_readers)
    p_u.seed_all(p1_psum_readers)
    p_d.seed_all(p1_psum_readers)

    wmat_rel = {("g", 0): [], ("g", 1): [], ("u", 0): [], ("u", 1): [],
                ("d", 0): [], ("d", 1): []}
    fuse_rel2 = [[], []]
    fuse_evs_by_k = {}
    acc_tile_ev = [None] * NTILE
    NK = 16 * 8
    conv_by_pass = {}

    def rec_w_chunk(p34, mat, ch):
        e_i, hf = divmod(p34, 2)
        par = p34 % 2
        src3, dstb, q_eng, q_sem, wsp = {
            "g": (wg_d, wg_bf[par], "sync", dW, ws_g),
            "u": (wu_d, wu_bf[par], "sync", dC, ws_u),
            "d": (wd_d, wd_bf[par], "sync", dD, ws_d)}[mat]
        if mat == "d":
            src_ap = src3[e_i, hf * 1024 + ch * 128: hf * 1024 + ch * 128 + 128, :]
        else:
            src_ap = src3[e_i, ch * 128:(ch + 1) * 128, hf * 1024:(hf + 1) * 1024]
        wsl, wst, wstw = wsp.acquire()
        ev_l = ctx.dma(q_eng, wst[:], src_ap,
                       waits=wstw + (wmat_rel[(mat, par)] if ch == 0 else []),
                       inc=q_sem)
        ev_c = ctx.op("gpsimd", lambda e, i=wst,
            o=dstb[:, ch * 1024:(ch + 1) * 1024]: e.tensor_copy(o, i[:]),
            waits=[ev_l] + (p1_end if p34 == 0 and ch == 0 and mat == "g" else []),
            inc=sG)
        wsp.release(wsl, ev_c)
        conv_by_pass.setdefault(p34, {}).setdefault(mat, []).append(ev_c)

    chunk_order = [(m, c) for c in range(8) for m in ("g", "u", "d")]
    for m, c in chunk_order:
        rec_w_chunk(0, m, c)

    def emit_xn_load(k):
        nsl, xnt, xnw = xn_sb.acquire()
        tc = k % 8
        ev_xn = None
        for c in range(8):
            ev_xn = ctx.dma("scalar", xnt[:, c * 512:(c + 1) * 512],
                xn_hbm[c, :, tc * 512: tc * 512 + 512],
                waits=[xn_group_ev[(tc * 512) // 1024]] + (xnw if c == 0 else []),
                inc=dN)
        return (nsl, xnt, ev_xn)

    def emit_acc_prefetch(k):
        p34, tc = divmod(k, 8)
        out = []
        for sub in range(4):
            ti = tc * 4 + sub
            asl, ai, aiw = acc_io.acquire()
            if p34 == 0:
                s0, t0 = divmod(ti * 128, T)
                ev_ai = ctx.dma("gpsimd", ai[:], x[s0, t0:t0 + 128, :],
                                waits=aiw, inc=dV)
            else:
                ev_ai = ctx.dma("gpsimd", ai[:], attn_acc[ti * 128:(ti + 1) * 128, :],
                                waits=[acc_tile_ev[ti]] + aiw, inc=dV)
            out.append((asl, ai, ev_ai))
        return out

    def emit_gu(k, xn_entry):
        p34, tc = divmod(k, 8)
        par = p34 % 2
        conv = conv_by_pass[p34]
        fpar = k % 2
        fb = fuse_bf2[fpar]
        frel, fuse_rel2[fpar] = fuse_rel2[fpar], []
        nsl, xnt, ev_xn = xn_entry
        fuse_evs = []
        for fc in range(8):
            gsl, pg, pgw = p_g.acquire()
            usl, pu, puw = p_u.acquire()
            for dc in range(8):
                ctx.op("tensor", lambda e, pg=pg,
                    lh=wg_bf[par][:, dc * 1024 + fc * 128: dc * 1024 + fc * 128 + 128],
                    rh=xnt[:, dc * 512:(dc + 1) * 512],
                    st=(dc == 0), sp=(dc == 7): e.matmul(
                    pg[:], lh, rh, start=st, stop=sp),
                    waits=([ev_xn, conv["g"][7]] + pgw) if dc == 0 else [],
                    inc=sT if dc == 7 else None)
            ev_g = (sT, sT.n)
            for dc in range(8):
                ctx.op("tensor", lambda e, pu=pu,
                    lh=wu_bf[par][:, dc * 1024 + fc * 128: dc * 1024 + fc * 128 + 128],
                    rh=xnt[:, dc * 512:(dc + 1) * 512],
                    st=(dc == 0), sp=(dc == 7): e.matmul(
                    pu[:], lh, rh, start=st, stop=sp),
                    waits=([conv["u"][7]] + puw) if dc == 0 else [],
                    inc=sT if dc == 7 else None)
            ev_u = (sT, sT.n)
            ssl, sl, slw = silu_sb.acquire()
            ev_sl = ctx.op("scalar", lambda e, pg=pg, sl=sl: e.activation(
                sl[:], pg[:], AF.Silu),
                reads=(pg, sl), writes=(sl,), waits=[ev_g] + slw, inc=sS)
            p_g.release(gsl, ev_sl)
            ev_f = ctx.op("vector", lambda e, pu=pu, sl=sl,
                o=fb[:, fc * 512:(fc + 1) * 512]: e.tensor_tensor(
                o, sl[:], pu[:], ALU.mult),
                reads=(sl, pu, fb), writes=(fb,),
                waits=[ev_u, ev_sl] + (frel if fc == 0 else []), inc=sV)
            p_u.release(usl, ev_f)
            silu_sb.release(ssl, ev_f)
            fuse_evs.append(ev_f)
        xn_sb.release(nsl, (sT, sT.n))
        fuse_evs_by_k[k] = fuse_evs
        if tc == 7:
            ev_gu_done = (sT, sT.n)
            wmat_rel[("g", par)] = [ev_gu_done]
            wmat_rel[("u", par)] = [ev_gu_done]

    def emit_down(k, acc_ins):
        p34, tc = divmod(k, 8)
        par = p34 % 2
        e_i = p34 // 2
        fpar = k % 2
        fb = fuse_bf2[fpar]
        fuse_evs = fuse_evs_by_k.pop(k)
        conv = conv_by_pass[p34]
        for sub in range(4):
            ti = tc * 4 + sub
            asl, ai, ev_ai = acc_ins[sub]
            osl, oo, oow = out_sb.acquire()
            ev_o_half = [None, None]
            for dhalf in range(2):
                dsl, pd, pdw = p_d.acquire()
                for fc in range(8):
                    ctx.op("tensor", lambda e, pd=pd,
                        lh=fb[:, fc * 512 + sub * 128: fc * 512 + sub * 128 + 128],
                        rh=wd_bf[par][:, fc * 1024 + dhalf * 512: fc * 1024 + dhalf * 512 + 512],
                        st=(fc == 0), sp=(fc == 7): e.matmul(
                        pd[:], lh, rh, start=st, stop=sp),
                        waits=([fuse_evs[fc]] +
                               (conv["d"][7:] if fc == 0 and dhalf == 0 and sub == 0 else []) +
                               (pdw if fc == 0 else [])),
                        inc=sT if fc == 7 else None)
                ev_dmm = (sT, sT.n)
                ev_o = ctx.op("vector", lambda e, pd=pd, ai=ai, oo=oo, dhalf=dhalf,
                    wcol=fullw_all[:, ti * 8 + e_i: ti * 8 + e_i + 1]:
                    e.scalar_tensor_tensor(
                    oo[:, dhalf * 512:(dhalf + 1) * 512], pd[:], wcol,
                    ai[:, dhalf * 512:(dhalf + 1) * 512], ALU.mult, ALU.add),
                    reads=(pd, ai, fullw_all, oo), writes=(oo,),
                    waits=[ev_dmm, ev_ai, fullw_ready[ti]] +
                          (oow if dhalf == 0 else []),
                    inc=sV)
                ev_o_half[dhalf] = ev_o
                p_d.release(dsl, ev_o)
            acc_io.release(asl, ev_o_half[1])
            ev_w = ctx.dma("gpsimd", attn_acc[ti * 128:(ti + 1) * 128, :], oo[:],
                           reads=(oo,), waits=ev_o_half, inc=dA)
            acc_tile_ev[ti] = ev_w
            out_sb.release(osl, ev_w)
        fuse_rel2[fpar] = [(sT, sT.n)]
        if tc == 7:
            wmat_rel[("d", par)] = [(sT, sT.n)]

    xn_cur = emit_xn_load(0)
    for k in range(NK):
        p34, tc = divmod(k, 8)
        acc_pf = emit_acc_prefetch(k - 1) if k > 0 else None
        xn_next = emit_xn_load(k + 1) if k + 1 < NK else None
        emit_gu(k, xn_cur)
        xn_cur = xn_next
        if k > 0:
            emit_down(k - 1, acc_pf)
        if tc < 2 and p34 + 1 < 2 * E:
            for m, c in chunk_order[tc * 12:(tc + 1) * 12]:
                rec_w_chunk(p34 + 1, m, c)
    emit_down(NK - 1, emit_acc_prefetch(NK - 1))

    p3_end = [(sV, sV.n), (sS, sS.n), (sT, sT.n), (sG, sG.n)]
    es3.close()

    ctx.barrier(("tensor", "vector", "scalar"), p3_end)

    es2 = ExitStack()
    v_aug = SB(es2, "v_aug", [128, 16 * 528], BF16)
    cprojT_bf = SB(es2, "cprojT_bf", [128, 8 * 1024], BF16)
    qkv_sb = RBuf([SB(es2, f"qkv_sb{i}", [128, 512], BF16) for i in range(3)])
    ql_sb = RBuf([SB(es2, f"ql_sb{i}", [64, 512], BF16) for i in range(2)])
    rope_f = RBuf([SB(es2, f"rope_f{i}", [64, 1024], F32) for i in range(2)])
    q4tmp = RBuf([SB(es2, f"q4tmp{i}", [128, 512], BF16) for i in range(1)])
    exp_sb = RBuf([SB(es2, f"exp_sb{i}", [128, 512], BF16) for i in range(3)])
    rcp_sb = RBuf([SB(es2, f"rcp_sb{i}", [33, 512], F32) for i in range(1)])
    bcast_sb = RBuf([SB(es2, f"bcast_sb{i}", [32, 512], F32) for i in range(2)])
    cos4 = SB(es2, "cos4_sb", [128, T], F32)
    sin4 = SB(es2, "sin4_sb", [128, T], F32)
    cprojb_sb = SB(es2, "cprojb_sb", [1, D], F32)

    pool = [PS(es2, f"bank{i}", [128, 512]) for i in range(8)]
    p_qkv = RBuf([pool[0], pool[1], pool[2]])
    p_lat = RBuf([pool[3], pool[7]])
    p_s = RBuf([pool[4], pool[5], pool[6]])
    p_yh = [pool[0], pool[1], pool[2], pool[3]]

    ev_cw = None
    for cc in range(8):
        wsl3, wst3, wstw3 = ws_g.acquire()
        ev_l = ctx.dma("sync", wst3[:], cprojT_d[cc * 128:(cc + 1) * 128, :],
                       waits=wstw3, inc=dW)
        ev_cw = ctx.op("gpsimd", lambda e, i=wst3, cc=cc,
            o=cprojT_bf[:, cc * 1024:(cc + 1) * 1024]: e.tensor_copy(o, i[:]),
            waits=[ev_l] + p3_end, inc=sG)
        ws_g.release(wsl3, ev_cw)
    ctx.dma("scalar", cos4[:], cos4_d, inc=dC)
    ev_cs = ctx.dma("scalar", sin4[:], sin4_d, inc=dC)
    ev_cpb = ctx.dma("scalar", cprojb_sb[:], cprojb_d, inc=dC)
    ev_vones = ctx.op("vector", lambda e: e.memset(
        v_aug[:].rearrange("p (kc h j) -> p kc h j", kc=16, h=16)[:, :, :, 32:33], 1.0),
        writes=(v_aug,), inc=sV)

    p2_prev = []
    p_yh_rel = [[] for _ in range(4)]
    pe_defer = []
    catw_rel = {0: [], 1: []}

    def rec_cat_chunk(pi, dc, joff, par2, conv_list):
        j0 = pi * D + joff
        cwb2 = catw_bf.tiles[par2]
        wsl, wst, wstw = ws_g.acquire()
        first = (pi == 0 and dc == 0)
        ev_l1 = ctx.dma("sync", wst[:, 0:128],
                        catT[dc * 128:(dc + 1) * 128, j0:j0 + 128],
                        waits=wstw + (catw_rel[par2] if first else []), inc=dW)
        wsl2, wst2, wstw2 = ws_u.acquire()
        ev_l2 = ctx.dma("sync", wst2[:, 0:128],
                        catT[dc * 128:(dc + 1) * 128, j0 + 128:j0 + 256],
                        waits=wstw2 + (catw_rel[par2] if first else []), inc=dW)
        if first:
            catw_rel[par2] = []
        ctx.op("gpsimd", lambda e, i=wst[:, 0:128],
            o=cwb2[:, pi * FF + dc * 256: pi * FF + dc * 256 + 128]:
            e.tensor_copy(o, i), waits=[ev_l1], inc=sG)
        ev = ctx.op("gpsimd", lambda e, i=wst2[:, 0:128],
            o=cwb2[:, pi * FF + dc * 256 + 128: pi * FF + dc * 256 + 256]:
            e.tensor_copy(o, i), waits=[ev_l2], inc=sG)
        ws_g.release(wsl, ev)
        ws_u.release(wsl2, ev)
        conv_list.append(ev)

    catw_carry = None
    for s in range(B_PER_CORE):
        if s > 0:
            ctx.barrier(("tensor", "vector", "scalar"), p2_prev)
        es2y = ExitStack()
        y4 = [SB(es2y, f"y4_{s}_{t}", [128, T], BF16, side="right") for t in range(4)]
        es2a = ExitStack()
        q4 = [SB(es2a, f"q4_{s}_{t}", [128, T], BF16) for t in range(4)]
        k4 = [SB(es2a, f"k4_{s}_{t}", [128, T], BF16) for t in range(4)]

        seq_fill = []
        cat_chunks = [(pi, dc) for pi in range(3) for dc in range(8)]
        if catw_carry is not None:
            catw_conv = {0: catw_carry}
        else:
            catw_conv = {0: []}
            for pi, dc in cat_chunks:
                rec_cat_chunk(pi, dc, 256 * 0, 0, catw_conv[0])

        for tau in range(4):
            conv_cur_cat = catw_conv[tau]
            cwb = catw_bf.tiles[tau % 2]
            catw_conv[tau + 1] = []

            for tc in range(4):
                if tau + 1 < 4 and tc < 2:
                    for pi, dc in cat_chunks[tc * 12:(tc + 1) * 12]:
                        rec_cat_chunk(pi, dc, 256 * (tau + 1), (tau + 1) % 2,
                                      catw_conv[tau + 1])
                nsl, xnt, xnw = xn_sb.acquire()
                ev_xn = None
                for c in range(8):
                    ev_xn = ctx.dma("scalar", xnt[:, c * 512:(c + 1) * 512],
                        xn_hbm[c, :, s * T + tc * 512: s * T + tc * 512 + 512],
                        waits=([xn_group_ev[(s * T + tc * 512) // 1024]] +
                               (xnw if c == 0 else [])),
                        inc=dN)

                ev_q4t = [None, None]
                for gg in range(2):
                    g = 2 * tau + gg
                    mmres = {}
                    for pi, part in enumerate(("q", "k", "v")):
                        qsl, pp, ppw = p_qkv.acquire()
                        for dc in range(8):
                            ctx.op("tensor", lambda e, pp=pp,
                                lh=cwb[:, pi * FF + dc * 256 + gg * 128:
                                        pi * FF + dc * 256 + gg * 128 + 128],
                                rh=xnt[:, dc * 512:(dc + 1) * 512],
                                st=(dc == 0), sp=(dc == 7): e.matmul(
                                pp[:], lh, rh, start=st, stop=sp),
                                waits=([conv_cur_cat[-1], ev_xn] if dc == 0 else []) +
                                      (ppw if dc == 0 else []),
                                inc=sT if dc == 7 else None)
                        mmres[part] = (qsl, pp, (sT, sT.n))
                    sbm = {}
                    for pi, part in enumerate(("q", "k", "v")):
                        qsl, pp, ev_mm = mmres[part]
                        ssl2, qsb, qsbw = qkv_sb.acquire()
                        col = {"q": g, "k": 8 + g, "v": 16 + g}[part]
                        ev_ev = ctx.op("scalar", lambda e, pp=pp, o=qsb,
                            b=catb_sb[:, col:col + 1]: e.activation(
                            o[:], pp[:], AF.Identity, bias=b),
                            reads=(pp, catb_sb, qsb), writes=(qsb,),
                            waits=[ev_mm, ld_done] + qsbw, inc=sS)
                        p_qkv.release(qsl, ev_ev)
                        sbm[part] = (qsb, ssl2, ev_ev)
                    latres = {}
                    for part in ("q", "k"):
                        qsb, qslot, ev_q = sbm[part]
                        wT = q2lT2_bf if part == "q" else kv2lT2_bf
                        lsl, pl, plw = p_lat.acquire()
                        ev_lat = ctx.op("tensor", lambda e, pl=pl, wT=wT, qsb=qsb:
                            e.matmul(pl[0:64, :], wT[:], qsb[:], start=True, stop=True),
                            waits=[ev_q, ev_smalls] + plw, inc=sT)
                        latres[part] = (lsl, pl, ev_lat, qslot)
                    qlres = {}
                    for part in ("q", "k"):
                        lsl, pl, ev_lat, qslot = latres[part]
                        bias = q2lb2 if part == "q" else kv2lb2
                        qlsl, qls, qlsw = ql_sb.acquire()
                        ev_ql = ctx.op("scalar", lambda e, pl=pl, o=qls, b=bias:
                            e.activation(o[:], pl[0:64, :], AF.Identity, bias=b[:]),
                            reads=(pl, bias, qls), writes=(qls,),
                            waits=[ev_lat, ld_done] + qlsw, inc=sS)
                        p_lat.release(lsl, ev_ql)
                        qkv_sb.release(qslot, ev_lat)
                        qlres[part] = (qlsl, qls, ev_ql)
                    rotres = {}
                    for part in ("q", "k"):
                        qlsl, qls, ev_ql = qlres[part]
                        rsl, pr, prw = p_lat.acquire()
                        ev_rot = ctx.op("tensor", lambda e, pr=pr, qls=qls: e.matmul(
                            pr[0:64, :], protT4[0:64, 0:64], qls[:],
                            start=True, stop=True),
                            waits=[ev_ql, ld_done] + prw, inc=sT)
                        rotres[part] = (rsl, pr, ev_rot, qlsl, qls, ev_ql)
                    vsb, vslot, ev_v = sbm["v"]
                    vres = []
                    for sub in range(4):
                        vsl, pv, pvw = p_qkv.acquire()
                        ctx.op("tensor", lambda e, pv=pv,
                            lh=vsb[:, sub * 128:(sub + 1) * 128]:
                            e.matmul(pv[:, 0:64], lh, kv2lT2_bf[:],
                                     start=True, stop=False),
                            waits=[ev_v, ev_smalls] + pvw)
                        ev_vb = ctx.op("tensor", lambda e, pv=pv: e.matmul(
                            pv[:, 0:64], ones_f[0:1, :], kv2lb2v[:],
                            start=False, stop=True),
                            waits=[ld_done], inc=sT)
                        vres.append((vsl, pv, ev_vb))
                    qkv_sb.release(vslot, vres[-1][2])
                    for part in ("q", "k"):
                        rsl, pr, ev_rot, qlsl, qls, ev_ql = rotres[part]
                        fsl, rpf, rpfw = rope_f.acquire()
                        ctx.op("vector", lambda e, qls=qls, o=rpf[:, 0:512],
                            c=cos4[0:64, tc * 512:(tc + 1) * 512]: e.tensor_tensor(
                            o, qls[:], c, ALU.mult),
                            reads=(qls, cos4, rpf), writes=(rpf,),
                            waits=[ev_ql, ev_cs] + rpfw)
                        ev_t2 = ctx.op("vector", lambda e, pr=pr, o=rpf[:, 512:1024],
                            sn=sin4[0:64, tc * 512:(tc + 1) * 512]: e.tensor_tensor(
                            o, pr[0:64, :], sn, ALU.mult),
                            reads=(pr, sin4, rpf), writes=(rpf,), waits=[ev_rot], inc=sV)
                        p_lat.release(rsl, ev_t2)
                        ql_sb.release(qlsl, ev_t2)
                        if part == "q":
                            if gg == 0:
                                qtsl, qt, qtw = q4tmp.acquire()
                            ev_ro = ctx.op("vector", lambda e, o=qt[64 * gg:64 * gg + 64, :],
                                a=rpf[:, 0:512], b=rpf[:, 512:1024]: e.tensor_tensor(
                                o, a, b, ALU.add),
                                reads=(rpf, qt), writes=(qt,),
                                waits=(qtw if gg == 0 else []), inc=sV)
                            ev_q4t[gg] = ev_ro
                        else:
                            ev_ro = ctx.op("vector", lambda e,
                                o=k4[tau][64 * gg:64 * gg + 64, tc * 512:(tc + 1) * 512],
                                a=rpf[:, 0:512], b=rpf[:, 512:1024]: e.tensor_tensor(
                                o, a, b, ALU.add),
                                reads=(rpf, k4[tau]), writes=(k4[tau],), inc=sV)
                            seq_fill.append(ev_ro)
                        rope_f.release(fsl, ev_ro)
                    for sub in range(4):
                        vsl, pv, ev_vb = vres[sub]
                        kc = tc * 4 + sub
                        ev_cp = None
                        for hh in range(2):
                            h2 = 2 * g + hh
                            ev_cp = ctx.op("scalar", lambda e,
                                o=v_aug[:, kc * 528 + h2 * 33: kc * 528 + h2 * 33 + 32],
                                i=pv[:, hh * 32:(hh + 1) * 32]: e.copy(o, i),
                                reads=(pv, v_aug), writes=(v_aug,),
                                waits=[ev_vb, ev_vones], inc=sS)
                        p_qkv.release(vsl, ev_cp)
                        seq_fill.append(ev_cp)

                wslb, pw, pww = p_s.acquire()
                ev_wqk = ctx.op("tensor", lambda e, pw=pw, qt=qt: e.matmul(
                    pw[:], wqkT4_bf[:], qt[:], start=True, stop=True),
                    waits=[ev_q4t[0], ev_q4t[1], ev_smalls] + pww, inc=sT)
                q4tmp.release(qtsl, ev_wqk)
                ev_q4 = ctx.op("scalar", lambda e, pw=pw,
                    o=q4[tau][:, tc * 512:(tc + 1) * 512]: e.activation(
                    o, pw[:], AF.Identity, bias=wqkb4[:]),
                    reads=(pw, wqkb4, q4[tau]), writes=(q4[tau],),
                    waits=[ev_wqk, ld_done], inc=sS)
                p_s.release(wslb, ev_q4)
                seq_fill.append(ev_q4)
                xn_sb.release(nsl, (sT, sT.n))
            catw_rel[tau % 2] = [(sT, sT.n)]

        first_score = True
        y4_evs = [[] for _ in range(4)]
        for h in range(H):
            tau, rb = h // 4, 32 * (h % 4)
            for qc in range(4):
                pyh_w, p_yh_rel[qc] = p_yh_rel[qc], []
                kcs = list(range(min(4 * qc + 4, 16)))
                pend = []
                ev_pv = None

                def emit_pv(entry, qc=qc):
                    kc, ex, esl, pv_wait, extra = entry
                    ev = ctx.op("tensor", lambda e, w=extra["w"], q0=extra["q0"],
                        lh=extra["lh"], rh=extra["rh"],
                        st=(kc == 0), sp=(kc == kcs[-1]): e.matmul(
                        p_yh[qc][0:33, q0 - 512 * qc: q0 - 512 * qc + w],
                        lh, rh, start=st, stop=sp),
                        waits=pv_wait + (pyh_w if kc == 0 else []), inc=sT)
                    exp_sb.release(esl, ev)
                    return ev

                for i, kc in enumerate(kcs):
                    q0 = max(512 * qc, 128 * kc)
                    w = 512 * qc + 512 - q0
                    psl, ps, psw = p_s.acquire()
                    ev_s = ctx.op("tensor", lambda e, ps=ps, w=w,
                        lh=k4[tau][rb:rb + 32, kc * 128:(kc + 1) * 128],
                        rh=q4[tau][rb:rb + 32, q0:q0 + w], rb=rb: e.matmul(
                        ps[:, 0:w], lh, rh, start=True, stop=True,
                        tile_position=(rb, 0)),
                        waits=(seq_fill if first_score else []) + psw, inc=sT)
                    first_score = False
                    if pe_defer:
                        pe_defer.pop(0)()
                    esl, ex, exw = exp_sb.acquire()
                    ev_e = ctx.op("scalar", lambda e, ps=ps, ex=ex, w=w: e.activation(
                        ex[:, 0:w], ps[:, 0:w], AF.Exp, scale=RSQL),
                        reads=(ps, ex), writes=(ex,), waits=[ev_s] + exw, inc=sS)
                    p_s.release(psl, ev_e)
                    pv_wait = [ev_e]
                    if q0 == 128 * kc:
                        ev_m = ctx.op("vector", lambda e, ex=ex: e.tensor_tensor(
                            ex[:, 0:128], ex[:, 0:128], maskdiag[:], ALU.mult),
                            reads=(ex, maskdiag), writes=(ex,),
                            waits=[ev_e, ld_done], inc=sV)
                        pv_wait = [ev_m]
                    rh_ap = ex[:, 0:w]
                    lh_ap = v_aug[:, kc * 528 + h * 33: kc * 528 + h * 33 + 33]
                    pend.append((kc, ex, esl, pv_wait,
                                 {"w": w, "q0": q0, "lh": lh_ap, "rh": rh_ap}))
                    if len(pend) > 2:
                        ev_pv = emit_pv(pend.pop(0))
                while pend:
                    ev_pv = emit_pv(pend.pop(0))

                rsl, rc, rcw = rcp_sb.acquire()
                ctx.op("scalar", lambda e, rc=rc, qc=qc: e.activation(
                    rc[32:33, :], p_yh[qc][32:33, :], AF.Ln),
                    reads=(p_yh[qc], rc), writes=(rc,), waits=[ev_pv] + rcw)
                ev_rc = ctx.op("scalar", lambda e, rc=rc: e.activation(
                    rc[32:33, :], rc[32:33, :], AF.Exp, scale=-1.0),
                    reads=(rc,), writes=(rc,), inc=sS)
                bsl, bc, bcw = bcast_sb.acquire()

                def emit_epi(qc=qc, tau=tau, rb=rb, rc=rc, rsl=rsl, bc=bc, bsl=bsl,
                             ev_rc=ev_rc, bcw=bcw):
                    ev_bc = ctx.op("tensor", lambda e, rc=rc, qc=qc: e.matmul(
                        p_yh[qc][64:96, :], ones_f[32:33, 0:32], rc[32:33, :],
                        start=True, stop=True, tile_position=(32, 64)),
                        waits=[ev_rc], inc=sT)
                    ev_bcs = ctx.op("scalar", lambda e, bc=bc, qc=qc: e.copy(
                        bc[:], p_yh[qc][64:96, :]),
                        reads=(p_yh[qc], bc), writes=(bc,),
                        waits=[ev_bc] + bcw, inc=sS)
                    rcp_sb.release(rsl, ev_bc)
                    ev_tt = ctx.op("vector", lambda e, qc=qc, bc=bc,
                        o=y4[tau][rb:rb + 32, qc * 512:(qc + 1) * 512]: e.tensor_tensor(
                        o, p_yh[qc][0:32, :], bc[:], ALU.mult),
                        reads=(p_yh[qc], bc, y4[tau]), writes=(y4[tau],),
                        waits=[ev_bcs], inc=sV)
                    bcast_sb.release(bsl, ev_tt)
                    p_yh_rel[qc].append(ev_tt)
                    y4_evs[tau].append(ev_tt)

                pe_defer.append(emit_epi)
        while pe_defer:
            pe_defer.pop(0)()

        es2a.close()
        es2b = ExitStack()
        y_fm = SB(es2b, f"y_fm_{s}", [128, 8 * T], BF16)

        yfm_evs = []
        for tau in range(4):
            for pp_i in range(2):
                for tc in range(4):
                    osl, po, pow_ = p_s.acquire()
                    ev_mm = ctx.op("tensor", lambda e, po=po, pp_i=pp_i,
                        lh=l2oT2_bf[64 * pp_i:64 * pp_i + 64, :],
                        rh=y4[tau][64 * pp_i:64 * pp_i + 64, tc * 512:(tc + 1) * 512]:
                        e.matmul(po[:], lh, rh, start=True, stop=True,
                                 tile_position=(64 * pp_i, 0)),
                        waits=y4_evs[tau] + pow_ + [ev_smalls], inc=sT)
                    cc = 2 * tau + pp_i
                    ev_ev = ctx.op("scalar", lambda e, po=po,
                        o=y_fm[:, cc * T + tc * 512: cc * T + tc * 512 + 512]:
                        e.activation(o, po[:], AF.Identity, bias=l2ob2[:]),
                        reads=(po, l2ob2, y_fm), writes=(y_fm,),
                        waits=[ev_mm, ld_done], inc=sS)
                    p_s.release(osl, ev_ev)
                    yfm_evs.append(ev_ev)

        es2y.close()

        po_seed = (sS, sS.n)
        p_o = RBuf([(pool[4], pool[5]), (pool[6], pool[7])])
        p_o.seed_all([po_seed])
        for tl in range(16):
            ti = s * 16 + tl
            asl, ai, aiw = acc_io.acquire()
            ev_ai = ctx.dma("scalar", ai[:], attn_acc[ti * 128:(ti + 1) * 128, :],
                            waits=[acc_tile_ev[ti]] + aiw, inc=dV)
            osl2, (poh0, poh1), pow_ = p_o.acquire()
            for half, ph in ((0, poh0), (1, poh1)):
                for cc in range(8):
                    ctx.op("tensor", lambda e, ph=ph,
                        lh=y_fm[:, cc * T + tl * 128: cc * T + tl * 128 + 128],
                        rh=cprojT_bf[:, cc * 1024 + half * 512: cc * 1024 + half * 512 + 512],
                        st=(cc == 0): e.matmul(ph[:], lh, rh, start=st, stop=False),
                        waits=(yfm_evs + [ev_cw] + pow_) if cc == 0 and half == 0 else [])
                ctx.op("tensor", lambda e, ph=ph,
                    rh=cprojb_sb[:, half * 512:(half + 1) * 512]: e.matmul(
                    ph[:], ones_f[0:1, :], rh, start=False, stop=True),
                    waits=[ev_cpb, ld_done], inc=sT if half == 1 else None)
            ev_bias = (sT, sT.n)
            osl3, oo2, oow2 = acc_io.acquire()
            ev_a0 = ctx.op("vector", lambda e, ph=poh0, oo=oo2, ai=ai: e.tensor_tensor(
                oo[:, 0:512], ph[:], ai[:, 0:512], ALU.add),
                reads=(poh0, ai, oo2), writes=(oo2,),
                waits=[ev_bias, ev_ai] + oow2, inc=sV)
            ev_a1 = ctx.op("vector", lambda e, ph=poh1, oo=oo2, ai=ai: e.tensor_tensor(
                oo[:, 512:1024], ph[:], ai[:, 512:1024], ALU.add),
                reads=(poh1, ai, oo2), writes=(oo2,), inc=sV)
            p_o.release(osl2, ev_a1)
            acc_io.release(asl, ev_a1)
            so, to = divmod(ti * 128, T)
            ev_st = ctx.dma("gpsimd", out_d[so, to:to + 128, :], oo2[:],
                            reads=(oo2,), waits=[ev_a0, ev_a1], inc=stO)
            acc_io.release(osl3, ev_st)

        if s + 1 < B_PER_CORE:
            catw_carry = []
            for pi, dc in cat_chunks:
                rec_cat_chunk(pi, dc, 0, 0, catw_carry)
        p2_prev = [(sV, sV.n), (sS, sS.n), (sT, sT.n), (sG, sG.n)]
        es2b.close()

    ctx.wait_only("gpsimd", [(stO, stO.n)])
    es2.close()

    with nc.Block() as block:
        @block.sync
        def _(h):
            ctx.replay("sync", h)

        @block.tensor
        def _(h):
            ctx.replay("tensor", h)

        @block.vector
        def _(h):
            ctx.replay("vector", h)

        @block.scalar
        def _(h):
            ctx.replay("scalar", h)

        @block.gpsimd
        def _(h):
            ctx.replay("gpsimd", h)

    es.close()
    return nc


def make_host_inputs(L0_attn_c_attn_w, L0_attn_c_attn_b, L0_attn_q2l_w,
                     L0_attn_q2l_b, L0_attn_kv2l_w, L0_attn_kv2l_b,
                     L0_attn_l2o_w, L0_attn_l2o_b, L0_attn_wqk_w, L0_attn_wqk_b,
                     L0_attn_cproj_w, L0_attn_cproj_b, L0_ffn_router_kernel_DE,
                     L0_ffn_kernel_gating_EDF, L0_ffn_kernel_up_EDF,
                     L0_ffn_kernel_down_EFD):
    f32 = np.float32
    bf16 = ml_dtypes.bfloat16
    C = lambda a: np.ascontiguousarray(np.asarray(a), dtype=f32)

    inv_freq = (1.0 / (f32(10000.0) ** (np.arange(0, L, 2, dtype=f32) / f32(L)))).astype(f32)
    t = np.arange(T, dtype=f32)
    freqs = np.outer(t, inv_freq).astype(f32)
    cos = np.repeat(np.cos(freqs).astype(f32), 2, axis=-1)[:, :L]
    sin = np.repeat(np.sin(freqs).astype(f32), 2, axis=-1)[:, :L]

    P = np.zeros((L, L), dtype=f32)
    for i in range(L // 2):
        P[2 * i, 2 * i + 1] = -1.0
        P[2 * i + 1, 2 * i] = 1.0

    q2lT = C(np.asarray(L0_attn_q2l_w).T)
    kv2lT = C(np.asarray(L0_attn_kv2l_w).T)
    l2oT = C(np.asarray(L0_attn_l2o_w).T)
    wqkT = C(np.asarray(L0_attn_wqk_w).T)
    l2oT2_blk = _blkdiag([l2oT, l2oT])

    return {
        "catT": C(np.asarray(L0_attn_c_attn_w).T),
        "catb_fm": C(np.asarray(L0_attn_c_attn_b).reshape(24, 128).T),
        "q2lT2": _blkdiag([q2lT, q2lT]),
        "kv2lT2": _blkdiag([kv2lT, kv2lT]),
        "q2lb2": C(np.tile(np.asarray(L0_attn_q2l_b), 2)[:, None]),
        "kv2lb2": C(np.tile(np.asarray(L0_attn_kv2l_b), 2)[:, None]),
        "kv2lb2v": C(np.tile(np.asarray(L0_attn_kv2l_b), 2)[None, :]),
        "wqkT4": _blkdiag([wqkT] * 4),
        "wqkb4": C(np.tile(np.asarray(L0_attn_wqk_b), 4)[:, None]),
        "protT4": _blkdiag([C(P.T)] * 4),
        "l2oT2": C(np.concatenate([l2oT2_blk, l2oT2_blk], axis=0)),
        "l2ob2": C(np.tile(np.asarray(L0_attn_l2o_b), 2)[:, None]),
        "cprojT": C(np.asarray(L0_attn_cproj_w).T),
        "cprojb": C(np.asarray(L0_attn_cproj_b)[None, :]),
        "router_w": C(L0_ffn_router_kernel_DE),
        "wg": C(L0_ffn_kernel_gating_EDF),
        "wu": C(L0_ffn_kernel_up_EDF),
        "wd": C(L0_ffn_kernel_down_EFD),
        "cos4": np.tile(C(cos.T), (4, 1)),
        "sin4": np.tile(C(sin.T), (4, 1)),
        
        "ident_f": np.eye(128, dtype=f32),
        "maskdiag": np.triu(np.ones((128, 128), dtype=f32)),
    }


def kernel(L0_attn_x, **w):
    nc = build_program()
    shared = make_host_inputs(**w)
    xs = np.asarray(L0_attn_x)
    in_maps = []
    for c in range(8):
        m = dict(shared)
        m["x"] = np.ascontiguousarray(xs[2 * c:2 * c + 2])
        in_maps.append(m)
    res = run_bass_kernel_spmd(nc, in_maps, core_ids=list(range(8)))
    return np.concatenate([res.results[c]["out"] for c in range(8)], axis=0)
